# version 02
# baseline (speedup 1.0000x reference)
import numpy as np
from contextlib import ExitStack

import concourse.bass as bass
import concourse.mybir as mybir
from concourse.bass_utils import run_bass_kernel_spmd

F32 = mybir.dt.float32
BF16 = mybir.dt.bfloat16
I32 = mybir.dt.int32
AF = mybir.ActivationFunctionType
OP = mybir.AluOpType
AX = mybir.AxisListType

D = 2048
DI = 4096
H = 64
P_ = 64
NS = 128
T = 1024
NCH = 16
XPD = 4416
NT = 8
ND = 16
NDI = 32

C15 = float(np.float32(1.5 * 2 ** 23))
XPROJ_OG = 8
XPROJ_J = 4
SSD_LIMIT = 16
CC_WORLD = 8
DEBUG = False


class PK:
    def __init__(self, nc):
        self.nc = nc
        self.q = {e: [] for e in ("tensor", "vector", "scalar", "gpsimd", "sync")}
        self.sems = {}
        self.cnt = {}

    def add_sem(self, name, handle):
        self.sems[name] = handle
        self.cnt[name] = 0

    def emit(self, eng, fn):
        self.q[eng].append(fn)

    def op(self, eng, fn, incs=()):
        incs = tuple(incs)
        for s, n in incs:
            self.cnt[s] += n

        def g(e, fn=fn, incs=incs, sems=self.sems):
            ins = fn(e)
            for s, n in incs:
                ins = ins.then_inc(sems[s], n)
        self.emit(eng, g)
        return {s: self.cnt[s] for s, _ in incs}

    def wait(self, eng, sem, val=None):
        v = self.cnt[sem] if val is None else val
        if v > 0:
            self.emit(eng, lambda e, s=self.sems[sem], v=v: e.wait_ge(s, v))

    def mark(self, eng, sem):
        if eng == "vector":
            return self.op(eng, lambda e: e.tensor_copy(self.junk_dst, self.junk_src), [(sem, 1)])[sem]
        if eng == "scalar":
            return self.op(eng, lambda e: e.copy(self.junk_dst3, self.junk_src), [(sem, 1)])[sem]
        if eng == "gpsimd":
            return self.op(eng, lambda e: e.memset(self.junk_dst2, 0.0), [(sem, 1)])[sem]
        return self.op(eng, lambda e: e.nop(), [(sem, 1)])[sem]

    def load(self, dst, src, sem="LD", slow=False):
        self.op("sync", lambda e, d=dst, s=src, sl=slow: e.dma_start(out=d, in_=s, allow_slow_non_contiguous=sl), [(sem, 16)])
        return self.cnt[sem]

    def store(self, dst, src):
        self.op("gpsimd", lambda e, d=dst, s=src: e.dma_start(out=d, in_=s), [("GD", 16)])
        return self.cnt["GD"]

    def gload(self, dst, src):
        self.op("gpsimd", lambda e, d=dst, s=src: e.dma_start(out=d, in_=s), [("GF", 16)])
        return self.cnt["GF"]


def build_program(stage=13):
    nc = bass.Bass("TRN2", target_bir_lowering=False, debug=False, detect_race_conditions=False)
    dt = nc.dram_tensor

    xin = dt("xin", [T, D], F32, kind="ExternalInput").ap()
    xprev = dt("xprev", [3, D], F32, kind="ExternalInput").ap()
    winT = dt("winT", [D, 8192], F32, kind="ExternalInput").ap()
    convw = dt("convw", [DI, 4], F32, kind="ExternalInput").ap()
    convb = dt("convb", [DI], F32, kind="ExternalInput").ap()
    wxT = dt("wxT", [DI, XPD], F32, kind="ExternalInput").ap()
    wdtT = dt("wdtT", [H, H], F32, kind="ExternalInput").ap()
    dtbias = dt("dtbias", [H], F32, kind="ExternalInput").ap()
    alog = dt("alog", [H], F32, kind="ExternalInput").ap()
    dvec = dt("dvec", [DI], F32, kind="ExternalInput").ap()
    woutT = dt("woutT", [DI, D], F32, kind="ExternalInput").ap()
    fwT = dt("fwT", [D, D], F32, kind="ExternalInput").ap()
    fbias = dt("fbias", [D], F32, kind="ExternalInput").ap()
    wsc = dt("wsc", [1, 1], F32, kind="ExternalInput").ap()
    selc = dt("selc", [128, 1], F32, kind="ExternalInput").ap()
    ident_i = dt("ident", [128, 128], F32, kind="ExternalInput").ap()
    uincl_i = dt("uincl", [64, 64], F32, kind="ExternalInput").ap()
    ustrict_i = dt("ustrict", [64, 64], F32, kind="ExternalInput").ap()
    mask64_i = dt("mask64", [64, 64], F32, kind="ExternalInput").ap()

    out = dt("out", [T, D], F32, kind="ExternalOutput").ap()

    def scr(name, shape):
        if DEBUG:
            return dt(name, shape, F32, kind="ExternalOutput").ap()
        return dt(name, shape, F32).ap()

    ln1T_d = scr("ln1T_d", [D, 1027])
    sz_d = scr("sz_d", [T, DI])
    X_d = scr("X_d", [T, DI])
    Y_d = scr("Y_d", [T, DI])
    YsT_d = scr("YsT_d", [DI, T])
    xmid_d = scr("xmid_d", [T, D])
    cflat_d = scr("cflat_d", [NCH, 4096])
    gflat_d = scr("gflat_d", [NCH, 4096])
    cdex_d = scr("cdex_d", [NCH, 4096])
    dbg_d = scr("dbg_d", [128, 8192]) if DEBUG else None
    smax_d = dt("smax_d", [256], F32).ap()
    ag_in_d = dt("ag_in_d", [128, 4096], F32).ap()
    ag_out_d = dt("ag_out_d", [256, 4096], F32).ap()
    mx_in_d = dt("mx_in_d", [1, 8], F32).ap()
    mx_out_d = dt("mx_out_d", [1, 8], F32).ap()
    sc_d = dt("sc_d", [1, 8], F32).ap()

    es = ExitStack()
    sbt = lambda name, shape, dty=F32: es.enter_context(nc.sbuf_tensor(name, shape, dty))
    pst = lambda name, shape: es.enter_context(nc.psum_tensor(name, shape, F32))

    REG_COLS = 33792
    reg_t = sbt("reg", [128, REG_COLS])
    consts = sbt("consts", [128, 1024])
    small = sbt("small", [128, 5632])
    smallB = sbt("smallB", [128, 4352])
    stg = sbt("stg", [128, 2048])
    stg2 = sbt("stg2", [128, 2048])
    state_t = sbt("state", [128, 4096])

    def R(off, cols, p0=0, pn=128):
        assert 0 <= off and off + cols <= REG_COLS, (off, cols)
        return reg_t[p0:p0 + pn, off:off + cols]

    c_ident = consts[:, 0:128]
    c_uincl = consts[0:64, 128:192]
    c_ustrict = consts[0:64, 192:256]
    c_mask64 = consts[0:64, 256:320]
    c_Arep = consts[:, 320:384]
    c_dtbrep = consts[:, 384:448]
    c_sel = consts[:, 448:449]
    c_eps = consts[:, 449:450]
    c_ws = consts[:, 450:451]
    c_invsc = consts[:, 452:453]
    c_convw = consts[:, 512:640]
    c_convb = consts[:, 640:672]
    c_junk = consts[:, 720:721]
    c_junksrc = consts[:, 721:722]
    c_m = lambda tt: consts[:, 672 + tt:673 + tt]
    c_rs = lambda tt: consts[:, 680 + tt:681 + tt]
    c_mx = consts[:, 688:689]
    c_mn = consts[:, 689:690]
    c_vv = consts[:, 690:691]
    c_t0 = consts[:, 691:692]
    c_t1 = consts[:, 692:693]
    c_gmx = consts[:, 693:694]
    c_m2 = lambda tt: consts[:, 700 + tt:701 + tt]
    c_rs2 = lambda tt: consts[:, 708 + tt:709 + tt]

    s_cAT = small[0:64, 0:1024]
    s_revT = small[0:64, 1024:2048]
    s_dt2c = small[0:64, 2048:3072]
    s_aTc = small[0:64, 3072:4096]
    s_cumA = small[0:64, 4096:5120]
    s_dsT = small[0:64, 5120:5184]
    s_cbm = small[0:64, 5184:5248]
    s_wdt = small[0:64, 5248:5312]

    s_BT = smallB[:, 0:1024]
    s_CT = smallB[:, 1024:2048]
    s_Btm = smallB[0:64, 2048:4096]
    s_chA = smallB[0:64, 4096:4112]
    s_off = smallB[0:64, 4112:4128]

    pBig0 = pst("pBig0", [128, 2048])
    pBig1 = pst("pBig1", [128, 2048])

    def bank(i, p0=0, pn=128, c0=0, cn=512):
        t = pBig0 if i < 4 else pBig1
        return t[p0:p0 + pn, (i % 4) * 512 + c0:(i % 4) * 512 + c0 + cn]

    k = PK(nc)
    k.junk_dst = consts[:, 720:721]
    k.junk_src = consts[0:128, 0:1]
    k.junk_dst2 = consts[:, 722:723]
    k.junk_dst3 = consts[:, 723:724]
    for s in ("LD", "LC", "LA0", "LA1", "LX", "LCD", "LW0", "LW1", "LR0", "LR1", "GD", "GF", "CC", "TE", "VE", "SE"):
        k.add_sem(s, es.enter_context(nc.semaphore(s)))
    LW = ("LW0", "LW1")
    LR = ("LR0", "LR1")

    def TT(e, o, a, b, op):
        return e.tensor_tensor(o, a, b, op)

    def vsep():
        k.op("vector", lambda e: e.tensor_copy(c_junk, c_ident[:, 0:1]))

    def phase_guard():
        k.wait("sync", "TE")
        k.wait("sync", "VE")
        k.wait("sync", "SE")

    def ssep():
        k.op("scalar", lambda e: e.copy(consts[:, 723:724], c_ident[:, 0:1]))

    k.load(c_ident, ident_i, sem="LC")
    k.load(c_uincl, uincl_i, sem="LC")
    k.load(c_ustrict, ustrict_i, sem="LC")
    k.load(c_mask64, mask64_i, sem="LC")
    k.load(c_Arep, alog.unsqueeze(0).broadcast_to([128, 64]), sem="LC")
    k.load(c_dtbrep, dtbias.unsqueeze(0).broadcast_to([128, 64]), sem="LC")
    k.load(c_sel, selc, sem="LC")
    k.load(c_ws, wsc.broadcast_to([128, 1]), sem="LC")
    k.load(c_convw.rearrange("p (o k) -> p o k", k=4), convw.rearrange("(o p) k -> p o k", p=128), sem="LC")
    k.load(c_convb, convb.rearrange("(o p) -> p o", p=128), sem="LC", slow=True)
    k.load(s_wdt, wdtT, sem="LC")
    k.wait("vector", "LC")
    k.op("vector", lambda e: e.memset(consts[:, 720:724], 0.0))
    k.op("vector", lambda e: e.memset(c_eps, 1e-5), [("VE", 1)])
    k.wait("scalar", "LC")
    k.op("scalar", lambda e: e.activation(stg[:, 1024:1088], c_Arep, AF.Exp), [("SE", 1)])
    k.wait("vector", "SE")
    k.op("vector", lambda e: e.tensor_scalar_mul(c_Arep, stg[:, 1024:1088], -1.0), [("VE", 1)])
    k.wait("tensor", "VE")
    k.wait("scalar", "VE")

    r_x = lambda par: R(0 + 2048 * par, 2048)
    r_dd = lambda par: R(4096 + 2048 * par, 2048)
    r_ts = lambda par: R(8192 + 2048 * par, 2048)

    def ln_tile(par, mcol, rscol, nrows=128):
        xt = r_x(par)[0:nrows, :]
        dtl = r_dd(par)[0:nrows, :]
        sq = r_ts(par)[0:nrows, :]
        k.op("vector", lambda e: e.reduce_sum(c_t0[0:nrows], xt, axis=AX.X))
        vsep()
        k.op("vector", lambda e: e.tensor_scalar_mul(mcol[0:nrows], c_t0[0:nrows], 1.0 / D))
        vsep()
        k.op("vector", lambda e: e.tensor_scalar_sub(dtl, xt, mcol[0:nrows]))
        k.op("vector", lambda e: TT(e, sq, dtl, dtl, OP.mult))
        k.op("vector", lambda e: e.reduce_sum(c_vv[0:nrows], sq, axis=AX.X))
        vsep()
        k.op("vector", lambda e: e.tensor_scalar_mul(c_vv[0:nrows], c_vv[0:nrows], 1.0 / D), [("VE", 1)])
        k.wait("scalar", "VE")
        k.op("scalar", lambda e: e.activation(c_t1[0:nrows], c_vv[0:nrows], AF.Sqrt, bias=c_eps[0:nrows]), [("SE", 1)])
        k.wait("vector", "SE")
        k.op("vector", lambda e: e.reciprocal(rscol[0:nrows], c_t1[0:nrows]))
        vsep()
        k.op("vector", lambda e: e.tensor_scalar_mul(dtl, dtl, rscol[0:nrows]), [("VE", 1)])
        return k.cnt["VE"]

    if stage >= 1:
        for tt in range(NT):
            par = tt % 2
            k.load(r_x(par), xin[tt * 128:(tt + 1) * 128, :], sem="LA0")
            k.wait("vector", "LA0")
            ve_ln = ln_tile(par, c_m(tt), c_rs(tt))
            k.wait("tensor", "VE", ve_ln)
            grp_ve = {0: 0, 1: 0}
            for g in range(4):
                bk = bank(g % 2)
                k.wait("tensor", "VE", grp_ve[g % 2])
                for j in range(4):
                    di = g * 4 + j
                    k.op("tensor", lambda e, bk=bk, j=j, di=di, par=par:
                         e.transpose(bk[:, j * 128:(j + 1) * 128],
                                     r_dd(par)[:, di * 128:(di + 1) * 128], c_ident), [("TE", 1)])
                k.wait("vector", "TE")
                k.op("vector", lambda e, bk=bk, g=g, par=par:
                     e.tensor_copy(r_ts(par)[:, g * 512:(g + 1) * 512], bk), [("VE", 1)])
                grp_ve[g % 2] = k.cnt["VE"]
            k.wait("gpsimd", "VE")
            k.store(ln1T_d.rearrange("(dt p) t -> p dt t", p=128)[:, :, 3 + tt * 128:3 + (tt + 1) * 128],
                    r_ts(par)[:].rearrange("p (dt c) -> p dt c", c=128))
            k.wait("sync", "GD")
        k.load(r_x(0)[0:3, :], xprev, sem="LA0")
        k.wait("vector", "LA0")
        ve_ln = ln_tile(0, c_t0, c_t1, nrows=3)
        k.wait("tensor", "VE", ve_ln)
        grp_ve = {0: 0, 1: 0}
        for g in range(4):
            bk = bank(g % 2)
            k.wait("tensor", "VE", grp_ve[g % 2])
            for j in range(4):
                di = g * 4 + j
                k.op("tensor", lambda e, bk=bk, j=j, di=di:
                     e.transpose(bk[:, j * 128:j * 128 + 3],
                                 r_dd(0)[0:3, di * 128:(di + 1) * 128], c_ident[0:3, 0:3]), [("TE", 1)])
            k.wait("vector", "TE")
            for j in range(4):
                k.op("vector", lambda e, bk=bk, g=g, j=j:
                     e.tensor_copy(r_ts(0)[:, (g * 4 + j) * 3:(g * 4 + j) * 3 + 3], bk[:, j * 128:j * 128 + 3]))
            k.mark("vector", "VE")
            grp_ve[g % 2] = k.cnt["VE"]
        k.wait("gpsimd", "VE")
        k.store(ln1T_d.rearrange("(dt p) t -> p dt t", p=128)[:, :, 0:3],
                r_ts(0)[:, 0:48].rearrange("p (dt c) -> p dt c", c=3))
        k.wait("sync", "GD")

    if stage >= 2:
        phase_guard()
        lnb = lambda tt: R(2048 * tt, 2048)
        zslab = lambda par: R(16384 + 8192 * par, 8192)
        for tt in range(NT):
            k.load(lnb(tt), ln1T_d.rearrange("(dt p) t -> p dt t", p=128)[:, :, 3 + tt * 128:3 + (tt + 1) * 128], sem="LA0")
        ln_ld = k.cnt["LA0"]
        slab_te = [0, 0]
        zstg_se = [0, 0]
        for op in range(8):
            par = op % 2
            k.wait("sync", "TE", slab_te[par])
            k.load(zslab(par).rearrange("p (dt o) -> p dt o", o=512),
                   winT.rearrange("(dt p) o -> p dt o", p=128)[:, :, 4096 + op * 512:4096 + (op + 1) * 512],
                   sem=LW[par])
            k.wait("tensor", LW[par])
            k.wait("tensor", "LA0", ln_ld)
            for tt in range(NT):
                bpar = tt % 2
                bk = bank(2 + bpar)
                k.wait("tensor", "SE", zstg_se[bpar])
                for di in range(ND):
                    k.op("tensor", lambda e, bk=bk, di=di, tt=tt, par=par:
                         e.matmul(bk, lnb(tt)[:, di * 128:(di + 1) * 128],
                                  zslab(par)[:, di * 512:(di + 1) * 512],
                                  start=(di == 0), stop=(di == ND - 1)),
                         [("TE", 1)] if di == ND - 1 else [])
                k.wait("scalar", "TE")
                k.wait("scalar", "GD")
                st_t = (stg if bpar == 0 else stg2)[:, 0:512]
                k.op("scalar", lambda e, st_t=st_t, bk=bk: e.activation(st_t, bk, AF.Silu), [("SE", 1)])
                zstg_se[bpar] = k.cnt["SE"]
                k.wait("gpsimd", "SE", zstg_se[bpar])
                k.store(sz_d[tt * 128:(tt + 1) * 128, op * 512:(op + 1) * 512], st_t)
            slab_te[par] = k.cnt["TE"]

    r_lnq = R(0, 8240)
    r_xbq = lambda ot: R(8240 + ot * 515, 515)
    r_wins = lambda par: R(24720 + 2048 * par, 2048)
    r_wxs = lambda par: R(24720, 8192) if par else R(0, 8192)
    r_carry = R(33280, 96)
    r_dtrawT = small[0:64, 3072:4096]

    if stage >= 3:
        for q in range(2):
            phase_guard()
            k.load(r_lnq.rearrange("p (dt w) -> p dt w", w=515),
                   ln1T_d.rearrange("(dt p) t -> p dt t", p=128)[:, :, q * 512:q * 512 + 515], sem="LA0")
            lnq_ld = k.cnt["LA0"]
            wins_te = [k.cnt["TE"], k.cnt["TE"]]
            evac_ve = {0: 0, 1: 0}
            for ot in range(NDI):
                par = ot % 2
                k.wait("sync", "TE", wins_te[par])
                k.load(r_wins(par).rearrange("p (dt o) -> p dt o", o=128),
                       winT.rearrange("(dt p) o -> p dt o", p=128)[:, :, ot * 128:(ot + 1) * 128],
                       sem=LW[par])
                bkA = bank(0 + (ot % 2))
                bkC = bank(2 + (ot % 2), c0=0, cn=3)
                k.wait("tensor", LW[par])
                if ot == 0:
                    k.wait("tensor", "LA0", lnq_ld)
                k.wait("tensor", "VE", evac_ve[ot % 2])
                for di in range(ND):
                    k.op("tensor", lambda e, bkA=bkA, di=di, par=par:
                         e.matmul(bkA, r_wins(par)[:, di * 128:(di + 1) * 128],
                                  r_lnq[:, di * 515 + 3:di * 515 + 515],
                                  start=(di == 0), stop=(di == ND - 1)))
                for di in range(ND):
                    k.op("tensor", lambda e, bkC=bkC, di=di, par=par:
                         e.matmul(bkC, r_wins(par)[:, di * 128:(di + 1) * 128],
                                  r_lnq[:, di * 515:di * 515 + 3],
                                  start=(di == 0), stop=(di == ND - 1)),
                         [("TE", 1)] if di == ND - 1 else [])
                wins_te[par] = k.cnt["TE"]
                k.wait("vector", "TE")
                k.op("vector", lambda e, ot=ot, bkA=bkA: e.tensor_copy(r_xbq(ot)[:, 3:515], bkA))
                if q == 0:
                    k.op("vector", lambda e, ot=ot, bkC=bkC: e.tensor_copy(r_xbq(ot)[:, 0:3], bkC), [("VE", 1)])
                else:
                    k.op("vector", lambda e, ot=ot, bkC=bkC: e.tensor_copy(r_xbq(ot)[:, 0:3], bkC))
                    k.op("vector", lambda e, ot=ot: e.tensor_copy(
                        r_xbq(ot)[:, 0:3], r_carry[:, ot * 3:ot * 3 + 3]), [("VE", 1)])
                evac_ve[ot % 2] = k.cnt["VE"]
            if q == 0:
                k.op("vector", lambda e: e.tensor_copy(
                    r_carry[:].rearrange("p (o c) -> p o c", c=3),
                    reg_t[:, 8240:8240 + 32 * 515].rearrange("p (o w) -> p o w", w=515)[:, :, 512:515]),
                    [("VE", 1)])

            for ot in range(NDI if stage >= 4 else 0):
                cw = lambda kk, ot=ot: c_convw[:, ot * 4 + kk:ot * 4 + kk + 1]
                cb = c_convb[:, ot:ot + 1]
                acc = stg[:, 0:512]
                if ot > 0:
                    k.wait("vector", "SE")
                k.op("vector", lambda e, ot=ot, cw=cw, acc=acc:
                     e.tensor_scalar(acc, r_xbq(ot)[:, 0:512], cw(0), None, OP.mult))
                for kk in range(1, 4):
                    k.op("vector", lambda e, ot=ot, kk=kk, cw=cw, acc=acc:
                         e.scalar_tensor_tensor(acc, r_xbq(ot)[:, kk:kk + 512], cw(kk), acc, OP.mult, OP.add))
                k.mark("vector", "VE")
                k.wait("scalar", "VE")
                k.op("scalar", lambda e, ot=ot, cb=cb, acc=acc:
                     e.activation(r_xbq(ot)[:, 3:515], acc, AF.Silu, bias=cb), [("SE", 1)])
            se_conv = k.cnt["SE"]
            if stage < 5:
                continue
            k.wait("tensor", "SE", se_conv)
            xcq = r_xbq
            wxs_te = [k.cnt["TE"], k.cnt["TE"]]
            bank47_ve = 0
            for ob in range(16):
                j = ob % 2
                par = ob % 2
                k.wait("sync", "TE", wxs_te[par])
                k.load(r_wxs(par).rearrange("p (dt o) -> p dt o", o=256),
                       wxT.rearrange("(dt p) o -> p dt o", p=128)[:, :, ob * 256:(ob + 1) * 256],
                       sem=LW[par])
                k.wait("tensor", LW[par])
                if j == 0:
                    k.wait("tensor", "VE", bank47_ve)
                for tq in range(4):
                    bk = bank(4 + tq, c0=j * 256, cn=256)
                    for di in range(NDI):
                        k.op("tensor", lambda e, bk=bk, di=di, tq=tq, par=par:
                             e.matmul(bk, xcq(di)[:, 3 + tq * 128:3 + (tq + 1) * 128],
                                      r_wxs(par)[:, di * 256:(di + 1) * 256],
                                      start=(di == 0), stop=(di == NDI - 1)),
                             [("TE", 1)] if (tq == 3 and di == NDI - 1) else [])
                wxs_te[par] = k.cnt["TE"]
                if j == 1:
                    og = ob // 2
                    k.wait("vector", "TE")
                    k.wait("vector", "GD")
                    for tq in range(4):
                        k.op("vector", lambda e, tq=tq: e.tensor_copy(stg2[:, tq * 512:(tq + 1) * 512], bank(4 + tq)), [("VE", 1)])
                    bank47_ve = k.cnt["VE"]
                    k.wait("gpsimd", "VE", bank47_ve)
                    k.store(
                        X_d.rearrange("(tq p) o -> p tq o", p=128)[:, 4 * q:4 * q + 4, og * 512:(og + 1) * 512],
                        stg2[:].rearrange("p (tq o) -> p tq o", o=512))
            for oi, (ocol0, m) in (enumerate([(4096, 64), (4160, 128), (4288, 128)]) if stage >= 6 else []):
                bk = bank(0 + oi % 2, pn=m)
                k.wait("sync", "TE")
                k.load(r_wxs(0)[:, 0:NDI * m].rearrange("p (dt o) -> p dt o", o=m),
                       wxT.rearrange("(dt p) o -> p dt o", p=128)[:, :, ocol0:ocol0 + m],
                       sem="LW0")
                k.wait("tensor", "LW0")
                k.wait("tensor", "VE")
                for di in range(NDI):
                    k.op("tensor", lambda e, bk=bk, di=di, m=m:
                         e.matmul(bk, r_wxs(0)[:, di * m:(di + 1) * m],
                                  xcq(di)[:, 3:515], start=(di == 0), stop=(di == NDI - 1)),
                         [("TE", 1)] if di == NDI - 1 else [])
                k.wait("vector", "TE")
                if oi == 0:
                    dst = r_dtrawT[0:64, q * 512:(q + 1) * 512]
                elif oi == 1:
                    dst = s_BT[:, q * 512:(q + 1) * 512]
                else:
                    dst = s_CT[:, q * 512:(q + 1) * 512]
                k.op("vector", lambda e, dst=dst, bk=bk: e.tensor_copy(dst, bk), [("VE", 1)])
            k.wait("sync", "VE")
            k.wait("sync", "TE")

    if DEBUG and 3 <= stage <= 4:
        k.wait("gpsimd", "VE")
        k.wait("gpsimd", "SE")
        for oo in range(4):
            k.store(dbg_d[:, oo * 515:(oo + 1) * 515], R(8240 + oo * 515, 515))
        k.wait("sync", "GD")

    if stage >= 6:
        phase_guard()
        k.wait("tensor", "VE")
        for c in range(NCH):
            bk = bank(0 + c % 2, pn=64, cn=128)
            if c >= 2:
                k.wait("tensor", "VE")
            k.op("tensor", lambda e, bk=bk, c=c: e.transpose(bk, s_BT[:, c * 64:(c + 1) * 64], c_ident), [("TE", 1)])
            k.wait("vector", "TE")
            k.op("vector", lambda e, bk=bk, c=c: e.tensor_copy(s_Btm[:, c * 128:(c + 1) * 128], bk), [("VE", 1)])
        for c in range(NCH):
            bk = bank(2 + c % 2, pn=64, cn=64)
            if c >= 2:
                k.wait("tensor", "VE")
            k.op("tensor", lambda e, bk=bk, c=c:
                 e.matmul(bk, r_dtrawT[0:64, c * 64:(c + 1) * 64], s_wdt, start=True, stop=True), [("TE", 1)])
            k.wait("vector", "TE")
            u = stg2[0:64, 0:64]
            z_ = stg2[0:64, 64:128]
            bb = stg2[0:64, 128:192]
            srs = stg2[0:64, 192:256]
            mm_ = stg2[0:64, 256:320]
            k.op("vector", lambda e, bk=bk, u=u: TT(e, u, bk, c_dtbrep[0:64, :], OP.add), [("VE", 1)])
            k.wait("scalar", "VE")
            k.op("scalar", lambda e, u=u, z_=z_: e.activation(z_, u, AF.Exp))
            k.op("scalar", lambda e, z_=z_, bb=bb: e.activation(bb, z_, AF.Ln, bias=1.0), [("SE", 1)])
            k.wait("vector", "SE")
            k.op("vector", lambda e, z_=z_, srs=srs: e.tensor_scalar(srs, z_, -0.2, 0.25, OP.mult, OP.add))
            for ck in (1.0 / 3, 0.5, 1.0):
                k.op("vector", lambda e, z_=z_, srs=srs: TT(e, srs, srs, z_, OP.mult))
                k.op("vector", lambda e, srs=srs, ck=ck: e.tensor_scalar(srs, srs, -1.0, ck, OP.mult, OP.add))
            k.op("vector", lambda e, z_=z_, srs=srs: TT(e, srs, srs, z_, OP.mult))
            k.op("vector", lambda e, u=u, mm_=mm_: e.tensor_single_scalar(mm_, u, -2.0, OP.is_lt))
            k.op("vector", lambda e, srs=srs, bb=bb: TT(e, srs, srs, bb, OP.subtract))
            k.op("vector", lambda e, srs=srs, mm_=mm_: TT(e, srs, srs, mm_, OP.mult))
            k.op("vector", lambda e, srs=srs, bb=bb, c=c: TT(e, s_dt2c[:, c * 64:(c + 1) * 64], srs, bb, OP.add))
            k.op("vector", lambda e, c=c: TT(e, s_aTc[:, c * 64:(c + 1) * 64],
                                             s_dt2c[:, c * 64:(c + 1) * 64], c_Arep[0:64, :], OP.mult), [("VE", 1)])
        ve_aT = k.cnt["VE"]
        k.wait("tensor", "VE", ve_aT)
        for c in range(NCH):
            aT_c = s_aTc[:, c * 64:(c + 1) * 64]
            bk = bank(0 + c % 2, pn=64)
            if c >= 2:
                k.wait("tensor", "VE")
            k.op("tensor", lambda e, bk=bk, aT_c=aT_c: e.matmul(bk[:, 0:64], c_uincl, aT_c, start=True, stop=True))
            k.op("tensor", lambda e, bk=bk, aT_c=aT_c: e.matmul(bk[:, 64:128], aT_c, c_uincl, start=True, stop=True))
            k.op("tensor", lambda e, bk=bk, aT_c=aT_c: e.matmul(bk[:, 128:192], c_ustrict, aT_c, start=True, stop=True), [("TE", 1)])
            k.wait("vector", "TE")
            k.op("vector", lambda e, bk=bk, c=c: e.tensor_copy(s_cAT[:, c * 64:(c + 1) * 64], bk[:, 0:64]))
            k.op("vector", lambda e, bk=bk, c=c: e.tensor_copy(s_cumA[:, c * 64:(c + 1) * 64], bk[:, 64:128]))
            k.op("vector", lambda e, bk=bk, c=c: e.tensor_copy(s_revT[:, c * 64:(c + 1) * 64], bk[:, 128:192]), [("VE", 1)])
        k.op("vector", lambda e: e.tensor_copy(s_chA, s_cumA[:, 63::64]))
        vsep()
        k.op("vector", lambda e: e.memset(s_off[:, 0:1], 0.0))
        vsep()
        for c in range(1, NCH):
            k.op("vector", lambda e, c=c: TT(e, s_off[:, c:c + 1], s_off[:, c - 1:c], s_chA[:, c - 1:c], OP.add))
            vsep()
        k.mark("vector", "VE")
        r_glob = R(0, 1024, pn=64)
        r_cde = R(1024, 1024, pn=64)
        k.op("vector", lambda e: TT(e, r_glob.rearrange("p (a b) -> p a b", b=64),
                                    s_cumA.rearrange("p (a b) -> p a b", b=64),
                                    s_off.unsqueeze(2).broadcast_to([64, 16, 64]), OP.add), [("VE", 1)])
        k.wait("scalar", "VE")
        k.op("scalar", lambda e: e.activation(stg[0:64, 0:16], s_chA, AF.Exp), [("SE", 1)])
        k.wait("vector", "SE")
        k.op("vector", lambda e: e.tensor_copy(
            r_cde.rearrange("p (a b) -> p a b", b=64),
            stg[0:64, 0:16].unsqueeze(2).broadcast_to([64, 16, 64])), [("VE", 1)])
        k.wait("gpsimd", "VE")
        k.store(cflat_d.rearrange("c (h t) -> h c t", h=64), s_cumA.rearrange("h (c t) -> h c t", c=16))
        k.store(gflat_d.rearrange("c (h t) -> h c t", h=64), r_glob.rearrange("h (c t) -> h c t", c=16))
        k.store(cdex_d.rearrange("c (h p) -> h c p", h=64), r_cde.rearrange("h (c p) -> h c p", c=16))
        k.wait("sync", "GD")

    if stage >= 7:
        phase_guard()
        r_rep = lambda par: R(4096 * par, 4096)
        r_ssL = R(8192, 4096, pn=64)
        r_cdec = R(12288, 4096)
        r_Xc = R(16384, 4096, pn=64)
        r_Xs = R(20480, 4096, pn=64)
        r_Yst = R(24576, 4096, pn=64)
        r_cdr = R(28672, 4096)

        k.op("vector", lambda e: e.memset(state_t[:], 0.0), [("VE", 1)])
        ve_state_done = k.cnt["VE"]
        rep_se = [0, 0]
        rep_ve = [0, 0]
        te_y_done = 0
        te_k_done = 0
        ve_evac = {0: 0, 1: 0}
        ve_xsd = 0
        se_ds = 0
        for c in range(SSD_LIMIT):
            par = c % 2
            k.wait("sync", "SE", rep_se[par])
            k.wait("sync", "VE", rep_ve[par])
            k.load(r_rep(par), cflat_d[c, :].unsqueeze(0).broadcast_to([128, 4096]), sem=LR[par])
            k.wait("sync", "TE", te_k_done)
            k.load(r_Xc, X_d[c * 64:(c + 1) * 64, :], sem="LX")
            xc_ld = k.cnt["LX"]
            k.wait("sync", "VE", ve_state_done)
            k.load(r_cdr, cdex_d[c, :].unsqueeze(0).broadcast_to([128, 4096]), sem="LCD")
            cdr_ld = k.cnt["LCD"]

            k.wait("scalar", LR[par])
            k.wait("scalar", "TE", te_y_done)
            k.wait("scalar", "VE")
            k.op("scalar", lambda e, par=par: e.activation(r_cdec, r_rep(par), AF.Exp), [("SE", 1)])
            rep_se[par] = k.cnt["SE"]
            k.wait("scalar", "VE", ve_xsd)
            k.op("scalar", lambda e, c=c: e.activation(s_dsT, s_revT[:, c * 64:(c + 1) * 64], AF.Exp), [("SE", 1)])
            se_ds = k.cnt["SE"]

            CT_c = s_CT[:, c * 64:(c + 1) * 64]
            BT_c = s_BT[:, c * 64:(c + 1) * 64]
            k.wait("vector", "SE", rep_se[par])
            k.op("vector", lambda e, CT_c=CT_c: TT(
                e, r_cdec.rearrange("p (h t) -> p h t", h=64),
                CT_c.unsqueeze(1).broadcast_to([128, 64, 64]),
                r_cdec.rearrange("p (h t) -> p h t", h=64), OP.mult), [("VE", 1)])

            k.wait("tensor", "VE")
            k.op("tensor", lambda e, BT_c=BT_c, CT_c=CT_c:
                 e.matmul(bank(6, pn=64, cn=64), BT_c, CT_c, start=True, stop=True), [("TE", 1)])
            te_cb = k.cnt["TE"]

            k.wait("vector", LR[par])
            k.wait("vector", "TE", te_y_done)
            cAT_c = s_cAT[:, c * 64:(c + 1) * 64]
            k.op("vector", lambda e, par=par, cAT_c=cAT_c: TT(
                e, r_ssL.rearrange("p (h t) -> p h t", h=64),
                r_rep(par)[0:64, :].rearrange("p (h t) -> p h t", h=64),
                cAT_c.unsqueeze(2).broadcast_to([64, 64, 64]), OP.subtract))
            k.op("vector", lambda e: e.tensor_scalar_min(r_ssL, r_ssL, 0.0), [("VE", 1)])
            rep_ve[par] = k.cnt["VE"]
            k.wait("scalar", "VE", rep_ve[par])
            k.op("scalar", lambda e: e.activation(r_ssL, r_ssL, AF.Exp), [("SE", 1)])
            se_L = k.cnt["SE"]
            k.wait("vector", "TE", te_cb)
            k.op("vector", lambda e: TT(e, s_cbm, bank(6, pn=64, cn=64), c_mask64, OP.mult))
            vsep()
            k.wait("vector", "SE", se_L)
            k.op("vector", lambda e: TT(
                e, r_ssL.rearrange("p (h t) -> p h t", h=64),
                r_ssL.rearrange("p (h t) -> p h t", h=64),
                s_cbm.unsqueeze(1).broadcast_to([64, 64, 64]), OP.mult), [("VE", 1)])
            if DEBUG and c == 8:
                k.wait("gpsimd", "VE")
                k.store(dbg_d[0:64, 4096:8192], r_ssL)
            k.wait("vector", "LX", xc_ld)
            dt2_c = s_dt2c[:, c * 64:(c + 1) * 64]
            k.op("vector", lambda e, dt2_c=dt2_c: TT(
                e, r_Xs.rearrange("p (h w) -> p h w", h=64),
                r_Xc.rearrange("p (h w) -> p h w", h=64),
                dt2_c.unsqueeze(2).broadcast_to([64, 64, 64]), OP.mult), [("VE", 1)])
            ve_mtxs = k.cnt["VE"]

            k.wait("tensor", "VE", ve_mtxs)
            k.wait("tensor", "VE", ve_state_done)
            for half in range(2):
                pb = pBig0 if half == 0 else pBig1
                k.wait("tensor", "VE", ve_evac[half])
                for j in range(32):
                    hh = half * 32 + j
                    k.op("tensor", lambda e, pb=pb, j=j, hh=hh:
                         e.matmul(pb[0:64, j * 64:(j + 1) * 64],
                                  r_ssL[:, hh * 64:(hh + 1) * 64],
                                  r_Xs[:, hh * 64:(hh + 1) * 64],
                                  start=True, stop=False))
                    k.op("tensor", lambda e, pb=pb, j=j, hh=hh:
                         e.matmul(pb[0:64, j * 64:(j + 1) * 64],
                                  r_cdec[:, hh * 64:(hh + 1) * 64],
                                  state_t[:, hh * 64:(hh + 1) * 64],
                                  start=False, stop=True),
                         [("TE", 1)] if j == 31 else [])
                te_half = k.cnt["TE"]
                k.wait("vector", "TE", te_half)
                k.wait("vector", "GD")
                k.op("vector", lambda e, pb=pb, half=half:
                     e.tensor_copy(r_Yst[:, half * 2048:(half + 1) * 2048], pb[0:64, :]), [("VE", 1)])
                ve_evac[half] = k.cnt["VE"]
            te_y_done = k.cnt["TE"]
            k.wait("gpsimd", "VE", ve_evac[1])
            k.store(Y_d[c * 64:(c + 1) * 64, :], r_Yst)

            k.wait("vector", "SE", se_ds)
            k.op("vector", lambda e: TT(
                e, r_Xs.rearrange("p (h w) -> p h w", h=64),
                r_Xs.rearrange("p (h w) -> p h w", h=64),
                s_dsT.unsqueeze(2).broadcast_to([64, 64, 64]), OP.mult), [("VE", 1)])
            ve_xsd = k.cnt["VE"]

            k.wait("tensor", "VE", ve_xsd)
            Btm_c = s_Btm[:, c * 128:(c + 1) * 128]
            for half in range(2):
                pb = pBig0 if half == 0 else pBig1
                k.wait("tensor", "VE", ve_evac[half])
                for pc in range(4):
                    k.op("tensor", lambda e, pb=pb, half=half, pc=pc, Btm_c=Btm_c:
                         e.matmul(pb[:, pc * 512:(pc + 1) * 512], Btm_c,
                                  r_Xs[:, half * 2048 + pc * 512:half * 2048 + (pc + 1) * 512],
                                  start=True, stop=True),
                         [("TE", 1)] if (half == 1 and pc == 3) else [])
            te_k_done = k.cnt["TE"]

            k.wait("vector", "LCD", cdr_ld)
            k.wait("vector", "TE", te_k_done)
            k.op("vector", lambda e: TT(e, state_t[:], state_t[:], r_cdr, OP.mult))
            k.op("vector", lambda e: TT(e, state_t[:, 0:2048], state_t[:, 0:2048], pBig0[:, :], OP.add))
            k.op("vector", lambda e: TT(e, state_t[:, 2048:4096], state_t[:, 2048:4096], pBig1[:, :], OP.add), [("VE", 1)])
            ve_state_done = k.cnt["VE"]
        if DEBUG:
            k.wait("gpsimd", "VE", ve_state_done)
            k.store(dbg_d[:, 0:4096], state_t[:])
        k.wait("sync", "GD")

    if stage >= 8:
        k.wait("gpsimd", "VE")
        k.store(ag_in_d, state_t[:])
        k.wait("gpsimd", "GD")
        k.op("gpsimd", lambda e: e.collective_compute(
            "AllGather", OP.bypass,
            replica_groups=[[0, 1], [2, 3], [4, 5], [6, 7]],
            ins=[ag_in_d.opt()], outs=[ag_out_d.opt()]), [("CC", 1)])
        k.wait("sync", "CC")
        k.load(state_t[:], ag_out_d[0:128, :])
        k.wait("vector", "LD")
        k.op("vector", lambda e: e.tensor_scalar_mul(state_t[:], state_t[:], c_sel), [("VE", 1)])
        ve_sin = k.cnt["VE"]

    if stage >= 9:
        r_Yh = R(0, 2048)
        r_Xh = R(2048, 2048)
        r_szh = R(4096, 2048)
        r_Dex = R(6144, 4096)
        r_rg = lambda par: R(10240 + 4096 * par, 4096)
        r_cg = R(18432, 4096)
        r_Ys = R(22528, 2048)
        r_Tst = R(24576, 2048)
        phase_guard()
        k.load(r_Dex, dvec.unsqueeze(0).broadcast_to([128, 4096]), sem="LA0")
        rg_se = [0, 0]
        cg_ve = 0
        te_corr = 0
        ys_ve = 0
        prev_tr_ve = 0
        for tt in range(NT):
            for half in range(2):
                hp0 = half * 2048
                k.wait("sync", "VE", ys_ve)
                k.load(r_Yh, Y_d[tt * 128:(tt + 1) * 128, hp0:hp0 + 2048], sem="LA0")
                k.load(r_Xh, X_d[tt * 128:(tt + 1) * 128, hp0:hp0 + 2048], sem="LA0")
                k.load(r_szh, sz_d[tt * 128:(tt + 1) * 128, hp0:hp0 + 2048], sem="LA0")
                ld_yxs = k.cnt["LA0"]
                for ci in range(2):
                    c = tt * 2 + ci
                    par = c % 2
                    k.wait("sync", "SE", rg_se[par])
                    k.load(r_rg(par), gflat_d[c, :].unsqueeze(0).broadcast_to([128, 4096]), sem=LR[par])
                    k.wait("scalar", LR[par])
                    k.wait("scalar", "VE", cg_ve)
                    k.wait("scalar", "TE", te_corr)
                    k.op("scalar", lambda e, par=par: e.activation(r_cg, r_rg(par), AF.Exp), [("SE", 1)])
                    rg_se[par] = k.cnt["SE"]
                    k.wait("vector", "SE", rg_se[par])
                    CT_c = s_CT[:, c * 64:(c + 1) * 64]
                    k.op("vector", lambda e, CT_c=CT_c: TT(
                        e, r_cg.rearrange("p (h t) -> p h t", h=64),
                        CT_c.unsqueeze(1).broadcast_to([128, 64, 64]),
                        r_cg.rearrange("p (h t) -> p h t", h=64), OP.mult), [("VE", 1)])
                    cg_ve = k.cnt["VE"]
                    k.wait("tensor", "VE", cg_ve)
                    if tt == 0 and half == 0 and ci == 0:
                        k.wait("tensor", "VE", ve_sin)
                    k.wait("tensor", "VE", ys_ve)
                    k.wait("tensor", "VE", prev_tr_ve)
                    pb = pBig0 if ci == 0 else pBig1
                    pofs = 0 if ci == 0 else 64
                    tp = None if ci == 0 else (0, 64)
                    for j in range(32):
                        hh = half * 32 + j
                        k.op("tensor", lambda e, pb=pb, j=j, hh=hh, pofs=pofs, tp=tp:
                             e.matmul(pb[pofs:pofs + 64, j * 64:(j + 1) * 64],
                                      r_cg[:, hh * 64:(hh + 1) * 64],
                                      state_t[:, hh * 64:(hh + 1) * 64],
                                      start=True, stop=True,
                                      tile_position=tp),
                             [("TE", 1)] if j == 31 else [])
                    te_corr = k.cnt["TE"]
                k.wait("vector", "LA0", ld_yxs)
                k.wait("vector", "TE", te_corr)
                for ci in range(2):
                    pb = pBig0 if ci == 0 else pBig1
                    pofs = 0 if ci == 0 else 64
                    rows = slice(ci * 64, ci * 64 + 64)
                    k.op("vector", lambda e, rows=rows, hp0=hp0: TT(
                        e, r_Ys[rows, :], r_Xh[rows, :], r_Dex[rows, hp0:hp0 + 2048], OP.mult))
                    k.op("vector", lambda e, rows=rows: TT(e, r_Ys[rows, :], r_Ys[rows, :], r_Yh[rows, :], OP.add))
                    k.op("vector", lambda e, rows=rows, pb=pb, pofs=pofs: TT(
                        e, r_Ys[rows, :], r_Ys[rows, :], pb[pofs:pofs + 64, :], OP.add))
                    k.op("vector", lambda e, rows=rows: TT(e, r_Ys[rows, :], r_Ys[rows, :], r_szh[rows, :], OP.mult), [("VE", 1)])
                ys_ve = k.cnt["VE"]
                k.wait("tensor", "VE", ys_ve)
                grp_ve = {0: 0, 1: 0}
                for g in range(4):
                    bk = bank(6 + g % 2)
                    k.wait("tensor", "VE", grp_ve[g % 2])
                    for j in range(4):
                        di = g * 4 + j
                        k.op("tensor", lambda e, bk=bk, j=j, di=di:
                             e.transpose(bk[:, j * 128:(j + 1) * 128], r_Ys[:, di * 128:(di + 1) * 128], c_ident), [("TE", 1)])
                    k.wait("vector", "TE")
                    k.wait("vector", "GD")
                    k.op("vector", lambda e, bk=bk, g=g: e.tensor_copy(r_Tst[:, g * 512:(g + 1) * 512], bk), [("VE", 1)])
                    grp_ve[g % 2] = k.cnt["VE"]
                ve_tr = k.cnt["VE"]
                prev_tr_ve = ve_tr
                k.wait("gpsimd", "VE", ve_tr)
                k.store(YsT_d.rearrange("(dt p) t -> p dt t", p=128)[:, half * 16:half * 16 + 16, tt * 128:(tt + 1) * 128],
                        r_Tst[:].rearrange("p (dt c) -> p dt c", c=128))
        k.wait("sync", "GD")

    if stage >= 10:
        phase_guard()
        r_ysth = R(0, 16384)
        r_wsl = lambda par: R(16384 + 4096 * par, 4096)
        wsl_te = [0, 0]
        xm_ve = {0: 0, 1: 0, 2: 0, 3: 0}
        for th in range(2):
            if th > 0:
                k.wait("sync", "TE")
            k.load(r_ysth.rearrange("p (dt t) -> p dt t", t=512),
                   YsT_d.rearrange("(dt p) t -> p dt t", p=128)[:, :, th * 512:(th + 1) * 512],
                   sem="LR0")
            for opi in range(16):
                par = opi % 2
                k.wait("sync", "TE", wsl_te[par])
                k.load(r_wsl(par).rearrange("p (dt o) -> p dt o", o=128),
                       woutT.rearrange("(dt p) o -> p dt o", p=128)[:, :, opi * 128:(opi + 1) * 128],
                       sem=LW[par])
                k.wait("tensor", LW[par])
                if opi == 0:
                    k.wait("tensor", "LR0")
                for tq in range(4):
                    tt = th * 4 + tq
                    bk = bank(tq, c0=(opi % 4) * 128, cn=128)
                    if opi % 4 == 0:
                        k.wait("tensor", "VE", xm_ve[tq])
                    for di in range(NDI):
                        k.op("tensor", lambda e, bk=bk, di=di, tq=tq, par=par:
                             e.matmul(bk, r_ysth[:, di * 512 + tq * 128:di * 512 + (tq + 1) * 128],
                                      r_wsl(par)[:, di * 128:(di + 1) * 128],
                                      start=(di == 0), stop=(di == NDI - 1)),
                             [("TE", 1)] if (tq == 3 and di == NDI - 1) else [])
                wsl_te[par] = k.cnt["TE"]
                if opi % 4 == 3:
                    te_b = k.cnt["TE"]
                    for tq in range(4):
                        tt = th * 4 + tq
                        k.wait("sync", "VE", xm_ve[tq])
                        k.load(stg[:, tq * 512:tq * 512 + 512],
                               xin[tt * 128:(tt + 1) * 128, (opi - 3) * 128:(opi + 1) * 128], sem="LA0")
                        ld_x = k.cnt["LA0"]
                        k.wait("vector", "TE", te_b)
                        k.wait("vector", "LA0", ld_x)
                        k.wait("vector", "GD")
                        k.op("vector", lambda e, tq=tq: TT(e, stg2[:, tq * 512:tq * 512 + 512],
                                                           bank(tq), stg[:, tq * 512:tq * 512 + 512], OP.add), [("VE", 1)])
                        xm_ve[tq] = k.cnt["VE"]
                        k.wait("gpsimd", "VE", xm_ve[tq])
                        k.store(xmid_d[tt * 128:(tt + 1) * 128, (opi - 3) * 128:(opi + 1) * 128],
                                stg2[:, tq * 512:tq * 512 + 512])
        k.wait("sync", "GD")

    if stage >= 11:
        phase_guard()
        k.op("vector", lambda e: e.memset(c_gmx, 0.0), [("VE", 1)])
        for tt in range(NT):
            par = tt % 2
            k.wait("sync", "SE")
            k.load(r_x(par), xmid_d[tt * 128:(tt + 1) * 128, :], sem="LA0")
            k.wait("vector", "LA0")
            xt = r_x(par)
            k.op("vector", lambda e, xt=xt: e.reduce_sum(c_t0, xt, axis=AX.X))
            vsep()
            k.op("vector", lambda e, tt=tt: e.tensor_scalar_mul(c_m2(tt), c_t0, 1.0 / D))
            k.op("vector", lambda e, xt=xt: e.reduce_max(c_mx, xt, axis=AX.X))
            k.op("vector", lambda e, xt=xt: e.tensor_reduce(c_mn, xt, AX.X, OP.min))
            vsep()
            k.op("vector", lambda e, xt=xt, tt=tt, par=par: e.tensor_scalar_sub(r_dd(par), xt, c_m2(tt)))
            k.op("vector", lambda e, par=par: TT(e, r_ts(par), r_dd(par), r_dd(par), OP.mult))
            k.op("vector", lambda e, par=par: e.reduce_sum(c_vv, r_ts(par), axis=AX.X))
            vsep()
            k.op("vector", lambda e: e.tensor_scalar_mul(c_vv, c_vv, 1.0 / D), [("VE", 1)])
            k.wait("scalar", "VE")
            k.op("scalar", lambda e: e.activation(c_t1, c_vv, AF.Sqrt, bias=c_eps), [("SE", 1)])
            k.wait("vector", "SE")
            k.op("vector", lambda e, tt=tt: e.reciprocal(c_rs2(tt), c_t1))
            vsep()
            k.op("vector", lambda e, tt=tt: TT(e, c_mx, c_mx, c_m2(tt), OP.subtract))
            vsep()
            k.op("vector", lambda e, tt=tt: TT(e, c_mn, c_m2(tt), c_mn, OP.subtract))
            vsep()
            k.op("vector", lambda e: TT(e, c_mx, c_mx, c_mn, OP.max))
            vsep()
            k.op("vector", lambda e, tt=tt: TT(e, c_mx, c_mx, c_rs2(tt), OP.mult))
            vsep()
            k.op("vector", lambda e: TT(e, c_gmx, c_gmx, c_mx, OP.max), [("VE", 1)])
        k.wait("gpsimd", "VE")
        k.store(smax_d[0:128], c_gmx)
        k.wait("sync", "GD")
        k.load(stg[0:1, 0:128], smax_d[0:128].unsqueeze(0))
        k.wait("vector", "LD")
        k.op("vector", lambda e: e.reduce_max(stg[0:1, 128:129], stg[0:1, 0:128], axis=AX.X))
        vsep()
        k.op("vector", lambda e: e.tensor_copy(stg[0:1, 136:144], stg[0:1, 128:129].broadcast_to([1, 8])), [("VE", 1)])
        k.wait("gpsimd", "VE")
        k.store(mx_in_d, stg[0:1, 136:144])
        k.wait("gpsimd", "GD")
        k.op("gpsimd", lambda e: e.collective_compute(
            "AllReduce", OP.max, replica_groups=[list(range(CC_WORLD))],
            ins=[mx_in_d.opt()], outs=[mx_out_d.opt()]), [("CC", 1)])
        k.wait("sync", "CC")
        k.load(stg[0:1, 144:152], mx_out_d)
        k.wait("vector", "LD")
        k.op("vector", lambda e: e.tensor_scalar(stg[0:1, 152:153], stg[0:1, 144:145], 1.0 / 448.0, 1e-7, OP.mult, OP.max))
        vsep()
        k.op("vector", lambda e: e.reciprocal(stg[0:1, 153:154], stg[0:1, 152:153]), [("VE", 1)])
        k.wait("gpsimd", "VE")
        k.store(sc_d[0:1, 0:1], stg[0:1, 153:154])
        k.wait("sync", "GD")
        k.load(c_invsc, sc_d[0:1, 0:1].broadcast_to([128, 1]))
        k.wait("vector", "LD")
        k.mark("vector", "VE")

    if stage >= 12:
        phase_guard()
        xq_area = R(0, 8192)
        s_xq = xq_area.bitcast(BF16)
        r_q = R(8192, 2048)
        r_xa = R(10240, 2048)
        r_da = R(12288, 2048)
        ve_q = 0
        for tt in range(NT):
            k.wait("sync", "VE", ve_q)
            k.load(r_xa, xmid_d[tt * 128:(tt + 1) * 128, :], sem="LA0")
            k.wait("vector", "LA0")
            k.op("vector", lambda e, tt=tt: e.tensor_scalar(r_da, r_xa, c_m2(tt), c_rs2(tt), OP.subtract, OP.mult), [("VE", 1)])
            k.wait("tensor", "VE")
            grp_ve = {0: 0, 1: 0}
            for g in range(4):
                bk = bank(0 + g % 2)
                k.wait("tensor", "VE", grp_ve[g % 2])
                for j in range(4):
                    di = g * 4 + j
                    k.op("tensor", lambda e, bk=bk, j=j, di=di:
                         e.transpose(bk[:, j * 128:(j + 1) * 128], r_da[:, di * 128:(di + 1) * 128], c_ident), [("TE", 1)])
                k.wait("vector", "TE")
                k.op("vector", lambda e, bk=bk, g=g: e.tensor_copy(r_q[:, g * 512:(g + 1) * 512], bk), [("VE", 1)])
                grp_ve[g % 2] = k.cnt["VE"]
            a = r_q
            ai = a.bitcast(I32)
            sb_ = stg[:, 0:2048]
            sbi = sb_.bitcast(I32)
            s2 = stg2[:, 0:2048]
            s2i = s2.bitcast(I32)
            k.op("vector", lambda e, a=a: e.tensor_scalar_mul(a, a, c_invsc))
            k.op("vector", lambda e, ai=ai, sbi=sbi: e.tensor_single_scalar(sbi, ai, 0x7F800000, OP.bitwise_and))
            k.op("vector", lambda e, sbi=sbi: e.tensor_single_scalar(sbi, sbi, 3 << 23, OP.subtract))
            k.op("vector", lambda e, sbi=sbi: e.tensor_single_scalar(sbi, sbi, 118 << 23, OP.max))
            k.op("vector", lambda e, s2i=s2i, sbi=sbi: e.tensor_scalar(s2i, sbi, -1, 0x7F000000, OP.mult, OP.add))
            k.op("vector", lambda e, a=a, s2=s2: TT(e, a, a, s2, OP.mult))
            k.op("vector", lambda e, a=a: e.tensor_scalar(a, a, C15, -C15, OP.add, OP.add))
            k.op("vector", lambda e, a=a, sb_=sb_: TT(e, a, a, sb_, OP.mult))
            k.op("vector", lambda e, tt=tt, a=a: e.tensor_copy(
                s_xq.rearrange("p (dt c) -> p dt c", c=1024)[:, :, tt * 128:(tt + 1) * 128],
                a.rearrange("p (dt c) -> p dt c", c=128)), [("VE", 1)])
            ve_q = k.cnt["VE"]
        if DEBUG:
            k.wait("gpsimd", "VE", ve_q)
            k.store(dbg_d[:, 4096:6144], xq_area[:, 0:2048])

    if stage >= 13:
        phase_guard()
        r_fs = lambda par: R(14336 + 4096 * par, 4096)
        r_fb = R(22528, 2048)
        k.load(r_fb, fbias.unsqueeze(0).broadcast_to([128, D]), sem="LC")
        fb_ld = k.cnt["LC"]
        fs_te = [0, 0]
        out_ve = {0: 0, 1: 0}
        ve_q_all = k.cnt["VE"]
        k.wait("tensor", "VE", ve_q_all)
        for opi in range(4):
            par = opi % 2
            k.wait("gpsimd", "TE", fs_te[par])
            fslab = r_fs(par).bitcast(BF16)
            k.op("gpsimd", lambda e, fslab=fslab, opi=opi: e.dma_start(
                out=fslab[:, 0:8192].rearrange("p (dt o) -> p dt o", o=512),
                in_=fwT.rearrange("(dt p) o -> p dt o", p=128)[:, :, opi * 512:(opi + 1) * 512]), [("GF", 16)])
            gf_ld = k.cnt["GF"]
            for tt in range(NT):
                bk = bank(0 + tt % 2)
                k.wait("tensor", "GF", gf_ld)
                k.wait("tensor", "VE", out_ve[tt % 2])
                for di in range(ND):
                    k.op("tensor", lambda e, bk=bk, di=di, tt=tt, fslab=fslab:
                         e.matmul(bk, s_xq.rearrange("p (dt c) -> p dt c", c=1024)[:, di, tt * 128:(tt + 1) * 128],
                                  fslab[:, di * 512:(di + 1) * 512],
                                  start=(di == 0), stop=(di == ND - 1)),
                         [("TE", 1)] if di == ND - 1 else [])
                te_f = k.cnt["TE"]
                xm_t = stg[:, 512 + (tt % 2) * 512:512 + (tt % 2) * 512 + 512]
                k.wait("sync", "VE", out_ve[tt % 2])
                k.load(xm_t, xmid_d[tt * 128:(tt + 1) * 128, opi * 512:(opi + 1) * 512], sem=("LA0" if tt % 2 == 0 else "LA1"))
                ld_xm = k.cnt["LA0" if tt % 2 == 0 else "LA1"]
                k.wait("vector", "TE", te_f)
                k.wait("vector", "LA0" if tt % 2 == 0 else "LA1", ld_xm)
                k.wait("vector", "LC", fb_ld)
                k.wait("vector", "GD")
                ot_t = stg2[:, (tt % 2) * 512:(tt % 2) * 512 + 512]
                k.op("vector", lambda e, bk=bk, xm_t=xm_t, ot_t=ot_t: e.scalar_tensor_tensor(
                    ot_t, bk, c_ws, xm_t, OP.mult, OP.add))
                k.op("vector", lambda e, opi=opi, ot_t=ot_t: TT(e, ot_t, ot_t, r_fb[:, opi * 512:(opi + 1) * 512], OP.add), [("VE", 1)])
                out_ve[tt % 2] = k.cnt["VE"]
                k.wait("gpsimd", "VE", out_ve[tt % 2])
                k.store(out[tt * 128:(tt + 1) * 128, opi * 512:(opi + 1) * 512], ot_t)
            fs_te[par] = k.cnt["TE"]

    k.wait("gpsimd", "GD")
    k.wait("gpsimd", "GF")

    with es:
        with nc.Block() as block:
            @block.sync
            def _(e):
                for f in k.q["sync"]:
                    f(e)

            @block.tensor
            def _(e):
                for f in k.q["tensor"]:
                    f(e)

            @block.vector
            def _(e):
                for f in k.q["vector"]:
                    f(e)

            @block.scalar
            def _(e):
                for f in k.q["scalar"]:
                    f(e)

            @block.gpsimd
            def _(e):
                for f in k.q["gpsimd"]:
                    f(e)
    return nc


def make_in_maps(inputs):
    f32 = np.float32
    x = np.ascontiguousarray(np.asarray(inputs["L0_attn_x"], f32))
    winT = np.ascontiguousarray(np.asarray(inputs["L0_attn_in_proj_w"], f32).T)
    convw = np.ascontiguousarray(np.asarray(inputs["L0_attn_conv_w"], f32))
    convb = np.ascontiguousarray(np.asarray(inputs["L0_attn_conv_b"], f32))
    wxT = np.ascontiguousarray(np.asarray(inputs["L0_attn_x_proj_w"], f32).T)
    wdtT = np.ascontiguousarray(np.asarray(inputs["L0_attn_dt_proj_w"], f32).T)
    dtb = np.ascontiguousarray(np.asarray(inputs["L0_attn_dt_proj_b"], f32))
    alog = np.ascontiguousarray(np.asarray(inputs["L0_attn_A_log"], f32))
    dvec = np.ascontiguousarray(np.repeat(np.asarray(inputs["L0_attn_D"], f32), P_))
    woutT = np.ascontiguousarray(np.asarray(inputs["L0_attn_out_proj_w"], f32).T)
    fwT = np.ascontiguousarray(np.asarray(inputs["L0_ffn_weight"], f32).T)
    fb = np.ascontiguousarray(np.asarray(inputs["L0_ffn_bias"], f32))
    wsc = np.asarray(inputs["L0_ffn_weight_scale"], f32).reshape(1, 1)
    ident = np.eye(128, dtype=f32)
    ar = np.arange(64)
    uincl = (ar[:, None] <= ar[None, :]).astype(f32)
    ustrict = (ar[:, None] > ar[None, :]).astype(f32)
    mask64 = (ar[None, :] >= ar[:, None]).astype(f32)
    maps = []
    for core in range(8):
        b, hh = core // 2, core % 2
        t0 = hh * T
        xi = np.ascontiguousarray(x[b, t0:t0 + T])
        xpr = np.ascontiguousarray(x[b, t0 - 3:t0]) if hh == 1 else np.zeros((3, D), f32)
        sel = np.full((128, 1), float(hh), f32)
        maps.append(dict(xin=xi, xprev=xpr, winT=winT, convw=convw, convb=convb,
                         wxT=wxT, wdtT=wdtT, dtbias=dtb, alog=alog, dvec=dvec,
                         woutT=woutT, fwT=fwT, fbias=fb, wsc=wsc, selc=sel,
                         ident=ident, uincl=uincl, ustrict=ustrict, mask64=mask64))
    return maps


def kernel(**inputs):
    nc = build_program(13)
    maps = make_in_maps(inputs)
    res = run_bass_kernel_spmd(nc, maps, core_ids=list(range(8)))
    outf = np.zeros((4, 2048, D), np.float32)
    for core in range(8):
        b, hh = core // 2, core % 2
        outf[b, hh * T:(hh + 1) * T] = res.results[core]["out"]
    return outf


# version 03
# speedup vs baseline: 1.0058x; 1.0058x over previous
import numpy as np
from contextlib import ExitStack

import concourse.bass as bass
import concourse.mybir as mybir
from concourse.bass_utils import run_bass_kernel_spmd

F32 = mybir.dt.float32
BF16 = mybir.dt.bfloat16
I32 = mybir.dt.int32
AF = mybir.ActivationFunctionType
OP = mybir.AluOpType
AX = mybir.AxisListType

D = 2048
DI = 4096
H = 64
P_ = 64
NS = 128
T = 1024
NCH = 16
XPD = 4416
NT = 8
ND = 16
NDI = 32

C15 = float(np.float32(1.5 * 2 ** 23))
XPROJ_OG = 8
XPROJ_J = 4
SSD_LIMIT = 16
CC_WORLD = 8
DEBUG = False


class PK:
    def __init__(self, nc):
        self.nc = nc
        self.q = {e: [] for e in ("tensor", "vector", "scalar", "gpsimd", "sync")}
        self.sems = {}
        self.cnt = {}

    def add_sem(self, name, handle):
        self.sems[name] = handle
        self.cnt[name] = 0

    def emit(self, eng, fn):
        self.q[eng].append(fn)

    def op(self, eng, fn, incs=()):
        incs = tuple(incs)
        for s, n in incs:
            self.cnt[s] += n

        def g(e, fn=fn, incs=incs, sems=self.sems):
            ins = fn(e)
            for s, n in incs:
                ins = ins.then_inc(sems[s], n)
        self.emit(eng, g)
        return {s: self.cnt[s] for s, _ in incs}

    def wait(self, eng, sem, val=None):
        v = self.cnt[sem] if val is None else val
        if v > 0:
            self.emit(eng, lambda e, s=self.sems[sem], v=v: e.wait_ge(s, v))

    def mark(self, eng, sem):
        if eng == "vector":
            return self.op(eng, lambda e: e.tensor_copy(self.junk_dst, self.junk_src), [(sem, 1)])[sem]
        if eng == "scalar":
            return self.op(eng, lambda e: e.copy(self.junk_dst3, self.junk_src), [(sem, 1)])[sem]
        if eng == "gpsimd":
            return self.op(eng, lambda e: e.memset(self.junk_dst2, 0.0), [(sem, 1)])[sem]
        return self.op(eng, lambda e: e.nop(), [(sem, 1)])[sem]

    def load(self, dst, src, sem="LD", slow=False):
        self.op("sync", lambda e, d=dst, s=src, sl=slow: e.dma_start(out=d, in_=s, allow_slow_non_contiguous=sl), [(sem, 16)])
        return self.cnt[sem]

    def store(self, dst, src):
        self.op("gpsimd", lambda e, d=dst, s=src: e.dma_start(out=d, in_=s), [("GD", 16)])
        return self.cnt["GD"]

    def gload(self, dst, src):
        self.op("gpsimd", lambda e, d=dst, s=src: e.dma_start(out=d, in_=s), [("GF", 16)])
        return self.cnt["GF"]


def build_program(stage=13):
    nc = bass.Bass("TRN2", target_bir_lowering=False, debug=False, detect_race_conditions=False)
    dt = nc.dram_tensor

    xin = dt("xin", [T, D], F32, kind="ExternalInput").ap()
    xprev = dt("xprev", [3, D], F32, kind="ExternalInput").ap()
    winT = dt("winT", [D, 8192], F32, kind="ExternalInput").ap()
    convw = dt("convw", [DI, 4], F32, kind="ExternalInput").ap()
    convb = dt("convb", [DI], F32, kind="ExternalInput").ap()
    wxT = dt("wxT", [DI, XPD], F32, kind="ExternalInput").ap()
    wdtT = dt("wdtT", [H, H], F32, kind="ExternalInput").ap()
    dtbias = dt("dtbias", [H], F32, kind="ExternalInput").ap()
    alog = dt("alog", [H], F32, kind="ExternalInput").ap()
    dvec = dt("dvec", [DI], F32, kind="ExternalInput").ap()
    woutT = dt("woutT", [DI, D], F32, kind="ExternalInput").ap()
    fwT = dt("fwT", [D, D], F32, kind="ExternalInput").ap()
    fbias = dt("fbias", [D], F32, kind="ExternalInput").ap()
    wsc = dt("wsc", [1, 1], F32, kind="ExternalInput").ap()
    selc = dt("selc", [128, 1], F32, kind="ExternalInput").ap()
    ident_i = dt("ident", [128, 128], F32, kind="ExternalInput").ap()
    uincl_i = dt("uincl", [64, 64], F32, kind="ExternalInput").ap()
    ustrict_i = dt("ustrict", [64, 64], F32, kind="ExternalInput").ap()
    mask64_i = dt("mask64", [64, 64], F32, kind="ExternalInput").ap()

    out = dt("out", [T, D], F32, kind="ExternalOutput").ap()

    def scr(name, shape):
        if DEBUG:
            return dt(name, shape, F32, kind="ExternalOutput").ap()
        return dt(name, shape, F32).ap()

    ln1T_d = scr("ln1T_d", [D, 1027])
    sz_d = scr("sz_d", [T, DI])
    X_d = scr("X_d", [T, DI])
    Y_d = scr("Y_d", [T, DI])
    YsT_d = scr("YsT_d", [DI, T])
    xmid_d = scr("xmid_d", [T, D])
    cflat_d = scr("cflat_d", [NCH, 4096])
    gflat_d = scr("gflat_d", [NCH, 4096])
    cdex_d = scr("cdex_d", [NCH, 4096])
    dbg_d = scr("dbg_d", [128, 8192]) if DEBUG else None
    smax_d = dt("smax_d", [256], F32).ap()
    ag_in_d = dt("ag_in_d", [128, 4096], F32).ap()
    ag_out_d = dt("ag_out_d", [256, 4096], F32).ap()
    mx_in_d = dt("mx_in_d", [1, 8], F32).ap()
    mx_out_d = dt("mx_out_d", [1, 8], F32).ap()
    sc_d = dt("sc_d", [1, 8], F32).ap()

    es = ExitStack()
    sbt = lambda name, shape, dty=F32: es.enter_context(nc.sbuf_tensor(name, shape, dty))
    pst = lambda name, shape: es.enter_context(nc.psum_tensor(name, shape, F32))

    REG_COLS = 33792
    reg_t = sbt("reg", [128, REG_COLS])
    consts = sbt("consts", [128, 1024])
    small = sbt("small", [128, 5632])
    smallB = sbt("smallB", [128, 4352])
    stg = sbt("stg", [128, 2048])
    stg2 = sbt("stg2", [128, 2048])
    state_t = sbt("state", [128, 4096])

    def R(off, cols, p0=0, pn=128):
        assert 0 <= off and off + cols <= REG_COLS, (off, cols)
        return reg_t[p0:p0 + pn, off:off + cols]

    c_ident = consts[:, 0:128]
    c_uincl = consts[0:64, 128:192]
    c_ustrict = consts[0:64, 192:256]
    c_mask64 = consts[0:64, 256:320]
    c_Arep = consts[:, 320:384]
    c_dtbrep = consts[:, 384:448]
    c_sel = consts[:, 448:449]
    c_eps = consts[:, 449:450]
    c_ws = consts[:, 450:451]
    c_invsc = consts[:, 452:453]
    c_convw = consts[:, 512:640]
    c_convb = consts[:, 640:672]
    c_junk = consts[:, 720:721]
    c_junksrc = consts[:, 721:722]
    c_m = lambda tt: consts[:, 672 + tt:673 + tt]
    c_rs = lambda tt: consts[:, 680 + tt:681 + tt]
    c_mx = consts[:, 688:689]
    c_mn = consts[:, 689:690]
    c_vv = consts[:, 690:691]
    c_t0 = consts[:, 691:692]
    c_t1 = consts[:, 692:693]
    c_gmx = consts[:, 693:694]
    c_m2 = lambda tt: consts[:, 700 + tt:701 + tt]
    c_rs2 = lambda tt: consts[:, 708 + tt:709 + tt]

    s_cAT = small[0:64, 0:1024]
    s_revT = small[0:64, 1024:2048]
    s_dt2c = small[0:64, 2048:3072]
    s_aTc = small[0:64, 3072:4096]
    s_cumA = small[0:64, 4096:5120]
    s_dsT = small[0:64, 5120:5184]
    s_cbm = small[0:64, 5184:5248]
    s_wdt = small[0:64, 5248:5312]

    s_BT = smallB[:, 0:1024]
    s_CT = smallB[:, 1024:2048]
    s_Btm = smallB[0:64, 2048:4096]
    s_chA = smallB[0:64, 4096:4112]
    s_off = smallB[0:64, 4112:4128]

    pBig0 = pst("pBig0", [128, 2048])
    pBig1 = pst("pBig1", [128, 2048])

    def bank(i, p0=0, pn=128, c0=0, cn=512):
        t = pBig0 if i < 4 else pBig1
        return t[p0:p0 + pn, (i % 4) * 512 + c0:(i % 4) * 512 + c0 + cn]

    k = PK(nc)
    k.junk_dst = consts[:, 720:721]
    k.junk_src = consts[0:128, 0:1]
    k.junk_dst2 = consts[:, 722:723]
    k.junk_dst3 = consts[:, 723:724]
    for s in ("LD", "LC", "LA0", "LA1", "LX", "LCD", "LW0", "LW1", "LR0", "LR1", "GD", "GF", "CC", "TE", "VE", "SE"):
        k.add_sem(s, es.enter_context(nc.semaphore(s)))
    LW = ("LW0", "LW1")
    LR = ("LR0", "LR1")

    def TT(e, o, a, b, op):
        return e.tensor_tensor(o, a, b, op)

    def vsep():
        k.op("vector", lambda e: e.tensor_copy(c_junk, c_ident[:, 0:1]))

    def phase_guard():
        k.wait("sync", "TE")
        k.wait("sync", "VE")
        k.wait("sync", "SE")

    def ssep():
        k.op("scalar", lambda e: e.copy(consts[:, 723:724], c_ident[:, 0:1]))

    k.load(c_ident, ident_i, sem="LC")
    k.load(c_uincl, uincl_i, sem="LC")
    k.load(c_ustrict, ustrict_i, sem="LC")
    k.load(c_mask64, mask64_i, sem="LC")
    k.load(c_Arep, alog.unsqueeze(0).broadcast_to([128, 64]), sem="LC")
    k.load(c_dtbrep, dtbias.unsqueeze(0).broadcast_to([128, 64]), sem="LC")
    k.load(c_sel, selc, sem="LC")
    k.load(c_ws, wsc.broadcast_to([128, 1]), sem="LC")
    k.load(c_convw.rearrange("p (o k) -> p o k", k=4), convw.rearrange("(o p) k -> p o k", p=128), sem="LC")
    k.load(c_convb, convb.rearrange("(o p) -> p o", p=128), sem="LC", slow=True)
    k.load(s_wdt, wdtT, sem="LC")
    k.wait("vector", "LC")
    k.op("vector", lambda e: e.memset(consts[:, 720:724], 0.0))
    k.op("vector", lambda e: e.memset(c_eps, 1e-5), [("VE", 1)])
    k.wait("scalar", "LC")
    k.op("scalar", lambda e: e.activation(stg[:, 1024:1088], c_Arep, AF.Exp), [("SE", 1)])
    k.wait("vector", "SE")
    k.op("vector", lambda e: e.tensor_scalar_mul(c_Arep, stg[:, 1024:1088], -1.0), [("VE", 1)])
    k.wait("tensor", "VE")
    k.wait("scalar", "VE")

    r_x = lambda par: R(0 + 2048 * par, 2048)
    r_dd = lambda par: R(4096 + 2048 * par, 2048)
    r_ts = lambda par: R(8192 + 2048 * par, 2048)

    def ln_tile(par, mcol, rscol, nrows=128):
        xt = r_x(par)[0:nrows, :]
        dtl = r_dd(par)[0:nrows, :]
        sq = r_ts(par)[0:nrows, :]
        k.op("vector", lambda e: e.reduce_sum(c_t0[0:nrows], xt, axis=AX.X))
        vsep()
        k.op("vector", lambda e: e.tensor_scalar_mul(mcol[0:nrows], c_t0[0:nrows], 1.0 / D))
        vsep()
        k.op("vector", lambda e: e.tensor_scalar_sub(dtl, xt, mcol[0:nrows]))
        k.op("vector", lambda e: TT(e, sq, dtl, dtl, OP.mult))
        k.op("vector", lambda e: e.reduce_sum(c_vv[0:nrows], sq, axis=AX.X))
        vsep()
        k.op("vector", lambda e: e.tensor_scalar_mul(c_vv[0:nrows], c_vv[0:nrows], 1.0 / D), [("VE", 1)])
        k.wait("scalar", "VE")
        k.op("scalar", lambda e: e.activation(c_t1[0:nrows], c_vv[0:nrows], AF.Sqrt, bias=c_eps[0:nrows]), [("SE", 1)])
        k.wait("vector", "SE")
        k.op("vector", lambda e: e.reciprocal(rscol[0:nrows], c_t1[0:nrows]))
        vsep()
        k.op("vector", lambda e: e.tensor_scalar_mul(dtl, dtl, rscol[0:nrows]), [("VE", 1)])
        return k.cnt["VE"]

    if stage >= 1:
        for tt in range(NT):
            par = tt % 2
            k.load(r_x(par), xin[tt * 128:(tt + 1) * 128, :], sem="LA0")
            k.wait("vector", "LA0")
            ve_ln = ln_tile(par, c_m(tt), c_rs(tt))
            k.wait("tensor", "VE", ve_ln)
            grp_ve = {0: 0, 1: 0}
            for g in range(4):
                bk = bank(g % 2)
                k.wait("tensor", "VE", grp_ve[g % 2])
                for j in range(4):
                    di = g * 4 + j
                    k.op("tensor", lambda e, bk=bk, j=j, di=di, par=par:
                         e.transpose(bk[:, j * 128:(j + 1) * 128],
                                     r_dd(par)[:, di * 128:(di + 1) * 128], c_ident), [("TE", 1)])
                k.wait("vector", "TE")
                k.op("vector", lambda e, bk=bk, g=g, par=par:
                     e.tensor_copy(r_ts(par)[:, g * 512:(g + 1) * 512], bk), [("VE", 1)])
                grp_ve[g % 2] = k.cnt["VE"]
            k.wait("gpsimd", "VE")
            k.store(ln1T_d.rearrange("(dt p) t -> p dt t", p=128)[:, :, 3 + tt * 128:3 + (tt + 1) * 128],
                    r_ts(par)[:].rearrange("p (dt c) -> p dt c", c=128))
            k.wait("sync", "GD")
        k.load(r_x(0)[0:3, :], xprev, sem="LA0")
        k.wait("vector", "LA0")
        ve_ln = ln_tile(0, c_t0, c_t1, nrows=3)
        k.wait("tensor", "VE", ve_ln)
        grp_ve = {0: 0, 1: 0}
        for g in range(4):
            bk = bank(g % 2)
            k.wait("tensor", "VE", grp_ve[g % 2])
            for j in range(4):
                di = g * 4 + j
                k.op("tensor", lambda e, bk=bk, j=j, di=di:
                     e.transpose(bk[:, j * 128:j * 128 + 3],
                                 r_dd(0)[0:3, di * 128:(di + 1) * 128], c_ident[0:3, 0:3]), [("TE", 1)])
            k.wait("vector", "TE")
            for j in range(4):
                k.op("vector", lambda e, bk=bk, g=g, j=j:
                     e.tensor_copy(r_ts(0)[:, (g * 4 + j) * 3:(g * 4 + j) * 3 + 3], bk[:, j * 128:j * 128 + 3]))
            k.mark("vector", "VE")
            grp_ve[g % 2] = k.cnt["VE"]
        k.wait("gpsimd", "VE")
        k.store(ln1T_d.rearrange("(dt p) t -> p dt t", p=128)[:, :, 0:3],
                r_ts(0)[:, 0:48].rearrange("p (dt c) -> p dt c", c=3))
        k.wait("sync", "GD")

    if stage >= 2:
        phase_guard()
        lnb = lambda tt: R(2048 * tt, 2048)
        zslab = lambda par: R(16384 + 8192 * par, 8192)
        for tt in range(NT):
            k.load(lnb(tt), ln1T_d.rearrange("(dt p) t -> p dt t", p=128)[:, :, 3 + tt * 128:3 + (tt + 1) * 128], sem="LA0")
        ln_ld = k.cnt["LA0"]
        slab_te = [0, 0]
        zstg_se = [0, 0]
        for op in range(8):
            par = op % 2
            k.wait("sync", "TE", slab_te[par])
            k.load(zslab(par).rearrange("p (dt o) -> p dt o", o=512),
                   winT.rearrange("(dt p) o -> p dt o", p=128)[:, :, 4096 + op * 512:4096 + (op + 1) * 512],
                   sem=LW[par])
            k.wait("tensor", LW[par])
            k.wait("tensor", "LA0", ln_ld)
            for tt in range(NT):
                bpar = tt % 2
                bk = bank(2 + bpar)
                k.wait("tensor", "SE", zstg_se[bpar])
                for di in range(ND):
                    k.op("tensor", lambda e, bk=bk, di=di, tt=tt, par=par:
                         e.matmul(bk, lnb(tt)[:, di * 128:(di + 1) * 128],
                                  zslab(par)[:, di * 512:(di + 1) * 512],
                                  start=(di == 0), stop=(di == ND - 1)),
                         [("TE", 1)] if di == ND - 1 else [])
                k.wait("scalar", "TE")
                k.wait("scalar", "GD")
                st_t = (stg if bpar == 0 else stg2)[:, 0:512]
                k.op("scalar", lambda e, st_t=st_t, bk=bk: e.activation(st_t, bk, AF.Silu), [("SE", 1)])
                zstg_se[bpar] = k.cnt["SE"]
                k.wait("gpsimd", "SE", zstg_se[bpar])
                k.store(sz_d[tt * 128:(tt + 1) * 128, op * 512:(op + 1) * 512], st_t)
            slab_te[par] = k.cnt["TE"]

    r_lnq = R(0, 8240)
    r_xbq = lambda ot: R(8240 + ot * 515, 515)
    r_wins = lambda par: R(24720 + 2048 * par, 2048)
    r_wxs = lambda par: R(24720, 8192) if par else R(0, 8192)
    r_carry = R(33280, 96)
    r_dtrawT = small[0:64, 3072:4096]

    if stage >= 3:
        for q in range(2):
            phase_guard()
            k.load(r_lnq.rearrange("p (dt w) -> p dt w", w=515),
                   ln1T_d.rearrange("(dt p) t -> p dt t", p=128)[:, :, q * 512:q * 512 + 515], sem="LA0")
            lnq_ld = k.cnt["LA0"]
            wins_te = [k.cnt["TE"], k.cnt["TE"]]
            evac_ve = {0: 0, 1: 0}
            for ot in range(NDI):
                par = ot % 2
                k.wait("sync", "TE", wins_te[par])
                k.load(r_wins(par).rearrange("p (dt o) -> p dt o", o=128),
                       winT.rearrange("(dt p) o -> p dt o", p=128)[:, :, ot * 128:(ot + 1) * 128],
                       sem=LW[par])
                bkA = bank(0 + (ot % 2))
                bkC = bank(2 + (ot % 2), c0=0, cn=3)
                k.wait("tensor", LW[par])
                if ot == 0:
                    k.wait("tensor", "LA0", lnq_ld)
                k.wait("tensor", "VE", evac_ve[ot % 2])
                for di in range(ND):
                    k.op("tensor", lambda e, bkA=bkA, di=di, par=par:
                         e.matmul(bkA, r_wins(par)[:, di * 128:(di + 1) * 128],
                                  r_lnq[:, di * 515 + 3:di * 515 + 515],
                                  start=(di == 0), stop=(di == ND - 1)))
                for di in range(ND):
                    k.op("tensor", lambda e, bkC=bkC, di=di, par=par:
                         e.matmul(bkC, r_wins(par)[:, di * 128:(di + 1) * 128],
                                  r_lnq[:, di * 515:di * 515 + 3],
                                  start=(di == 0), stop=(di == ND - 1)),
                         [("TE", 1)] if di == ND - 1 else [])
                wins_te[par] = k.cnt["TE"]
                k.wait("vector", "TE")
                k.op("vector", lambda e, ot=ot, bkA=bkA: e.tensor_copy(r_xbq(ot)[:, 3:515], bkA))
                if q == 0:
                    k.op("vector", lambda e, ot=ot, bkC=bkC: e.tensor_copy(r_xbq(ot)[:, 0:3], bkC), [("VE", 1)])
                else:
                    k.op("vector", lambda e, ot=ot, bkC=bkC: e.tensor_copy(r_xbq(ot)[:, 0:3], bkC))
                    k.op("vector", lambda e, ot=ot: e.tensor_copy(
                        r_xbq(ot)[:, 0:3], r_carry[:, ot * 3:ot * 3 + 3]), [("VE", 1)])
                evac_ve[ot % 2] = k.cnt["VE"]
            if q == 0:
                k.op("vector", lambda e: e.tensor_copy(
                    r_carry[:].rearrange("p (o c) -> p o c", c=3),
                    reg_t[:, 8240:8240 + 32 * 515].rearrange("p (o w) -> p o w", w=515)[:, :, 512:515]),
                    [("VE", 1)])

            for ot in range(NDI if stage >= 4 else 0):
                cw = lambda kk, ot=ot: c_convw[:, ot * 4 + kk:ot * 4 + kk + 1]
                cb = c_convb[:, ot:ot + 1]
                acc = stg[:, 0:512]
                if ot > 0:
                    k.wait("vector", "SE")
                k.op("vector", lambda e, ot=ot, cw=cw, acc=acc:
                     e.tensor_scalar(acc, r_xbq(ot)[:, 0:512], cw(0), None, OP.mult))
                for kk in range(1, 4):
                    k.op("vector", lambda e, ot=ot, kk=kk, cw=cw, acc=acc:
                         e.scalar_tensor_tensor(acc, r_xbq(ot)[:, kk:kk + 512], cw(kk), acc, OP.mult, OP.add))
                k.mark("vector", "VE")
                k.wait("scalar", "VE")
                k.op("scalar", lambda e, ot=ot, cb=cb, acc=acc:
                     e.activation(r_xbq(ot)[:, 3:515], acc, AF.Silu, bias=cb), [("SE", 1)])
            se_conv = k.cnt["SE"]
            if stage < 5:
                continue
            k.wait("tensor", "SE", se_conv)
            xcq = r_xbq
            wxs_te = [k.cnt["TE"], k.cnt["TE"]]
            bank47_ve = 0
            for ob in range(16):
                j = ob % 2
                par = ob % 2
                k.wait("sync", "TE", wxs_te[par])
                k.load(r_wxs(par).rearrange("p (dt o) -> p dt o", o=256),
                       wxT.rearrange("(dt p) o -> p dt o", p=128)[:, :, ob * 256:(ob + 1) * 256],
                       sem=LW[par])
                k.wait("tensor", LW[par])
                if j == 0:
                    k.wait("tensor", "VE", bank47_ve)
                for tq in range(4):
                    bk = bank(4 + tq, c0=j * 256, cn=256)
                    for di in range(NDI):
                        k.op("tensor", lambda e, bk=bk, di=di, tq=tq, par=par:
                             e.matmul(bk, xcq(di)[:, 3 + tq * 128:3 + (tq + 1) * 128],
                                      r_wxs(par)[:, di * 256:(di + 1) * 256],
                                      start=(di == 0), stop=(di == NDI - 1)),
                             [("TE", 1)] if (tq == 3 and di == NDI - 1) else [])
                wxs_te[par] = k.cnt["TE"]
                if j == 1:
                    og = ob // 2
                    k.wait("vector", "TE")
                    k.wait("vector", "GD")
                    for tq in range(4):
                        k.op("vector", lambda e, tq=tq: e.tensor_copy(stg2[:, tq * 512:(tq + 1) * 512], bank(4 + tq)), [("VE", 1)])
                    bank47_ve = k.cnt["VE"]
                    k.wait("gpsimd", "VE", bank47_ve)
                    k.store(
                        X_d.rearrange("(tq p) o -> p tq o", p=128)[:, 4 * q:4 * q + 4, og * 512:(og + 1) * 512],
                        stg2[:].rearrange("p (tq o) -> p tq o", o=512))
            for oi, (ocol0, m) in (enumerate([(4096, 64), (4160, 128), (4288, 128)]) if stage >= 6 else []):
                bk = bank(0 + oi % 2, pn=m)
                k.wait("sync", "TE")
                k.load(r_wxs(0)[:, 0:NDI * m].rearrange("p (dt o) -> p dt o", o=m),
                       wxT.rearrange("(dt p) o -> p dt o", p=128)[:, :, ocol0:ocol0 + m],
                       sem="LW0")
                k.wait("tensor", "LW0")
                k.wait("tensor", "VE")
                for di in range(NDI):
                    k.op("tensor", lambda e, bk=bk, di=di, m=m:
                         e.matmul(bk, r_wxs(0)[:, di * m:(di + 1) * m],
                                  xcq(di)[:, 3:515], start=(di == 0), stop=(di == NDI - 1)),
                         [("TE", 1)] if di == NDI - 1 else [])
                k.wait("vector", "TE")
                if oi == 0:
                    dst = r_dtrawT[0:64, q * 512:(q + 1) * 512]
                elif oi == 1:
                    dst = s_BT[:, q * 512:(q + 1) * 512]
                else:
                    dst = s_CT[:, q * 512:(q + 1) * 512]
                k.op("vector", lambda e, dst=dst, bk=bk: e.tensor_copy(dst, bk), [("VE", 1)])
            k.wait("sync", "VE")
            k.wait("sync", "TE")

    if DEBUG and 3 <= stage <= 4:
        k.wait("gpsimd", "VE")
        k.wait("gpsimd", "SE")
        for oo in range(4):
            k.store(dbg_d[:, oo * 515:(oo + 1) * 515], R(8240 + oo * 515, 515))
        k.wait("sync", "GD")

    if stage >= 6:
        phase_guard()
        k.wait("tensor", "VE")
        for c in range(NCH):
            bk = bank(0 + c % 2, pn=64, cn=128)
            if c >= 2:
                k.wait("tensor", "VE")
            k.op("tensor", lambda e, bk=bk, c=c: e.transpose(bk, s_BT[:, c * 64:(c + 1) * 64], c_ident), [("TE", 1)])
            k.wait("vector", "TE")
            k.op("vector", lambda e, bk=bk, c=c: e.tensor_copy(s_Btm[:, c * 128:(c + 1) * 128], bk), [("VE", 1)])
        for c in range(NCH):
            bk = bank(2 + c % 2, pn=64, cn=64)
            if c >= 2:
                k.wait("tensor", "VE")
            k.op("tensor", lambda e, bk=bk, c=c:
                 e.matmul(bk, r_dtrawT[0:64, c * 64:(c + 1) * 64], s_wdt, start=True, stop=True), [("TE", 1)])
            k.wait("vector", "TE")
            u = stg2[0:64, 0:64]
            z_ = stg2[0:64, 64:128]
            bb = stg2[0:64, 128:192]
            srs = stg2[0:64, 192:256]
            mm_ = stg2[0:64, 256:320]
            k.op("vector", lambda e, bk=bk, u=u: TT(e, u, bk, c_dtbrep[0:64, :], OP.add), [("VE", 1)])
            k.wait("scalar", "VE")
            k.op("scalar", lambda e, u=u, z_=z_: e.activation(z_, u, AF.Exp))
            k.op("scalar", lambda e, z_=z_, bb=bb: e.activation(bb, z_, AF.Ln, bias=1.0), [("SE", 1)])
            k.wait("vector", "SE")
            k.op("vector", lambda e, z_=z_, srs=srs: e.tensor_scalar(srs, z_, -0.2, 0.25, OP.mult, OP.add))
            for ck in (1.0 / 3, 0.5, 1.0):
                k.op("vector", lambda e, z_=z_, srs=srs: TT(e, srs, srs, z_, OP.mult))
                k.op("vector", lambda e, srs=srs, ck=ck: e.tensor_scalar(srs, srs, -1.0, ck, OP.mult, OP.add))
            k.op("vector", lambda e, z_=z_, srs=srs: TT(e, srs, srs, z_, OP.mult))
            k.op("vector", lambda e, u=u, mm_=mm_: e.tensor_single_scalar(mm_, u, -2.0, OP.is_lt))
            k.op("vector", lambda e, srs=srs, bb=bb: TT(e, srs, srs, bb, OP.subtract))
            k.op("vector", lambda e, srs=srs, mm_=mm_: TT(e, srs, srs, mm_, OP.mult))
            k.op("vector", lambda e, srs=srs, bb=bb, c=c: TT(e, s_dt2c[:, c * 64:(c + 1) * 64], srs, bb, OP.add))
            k.op("vector", lambda e, c=c: TT(e, s_aTc[:, c * 64:(c + 1) * 64],
                                             s_dt2c[:, c * 64:(c + 1) * 64], c_Arep[0:64, :], OP.mult), [("VE", 1)])
        ve_aT = k.cnt["VE"]
        k.wait("tensor", "VE", ve_aT)
        for c in range(NCH):
            aT_c = s_aTc[:, c * 64:(c + 1) * 64]
            bk = bank(0 + c % 2, pn=64)
            if c >= 2:
                k.wait("tensor", "VE")
            k.op("tensor", lambda e, bk=bk, aT_c=aT_c: e.matmul(bk[:, 0:64], c_uincl, aT_c, start=True, stop=True))
            k.op("tensor", lambda e, bk=bk, aT_c=aT_c: e.matmul(bk[:, 64:128], aT_c, c_uincl, start=True, stop=True))
            k.op("tensor", lambda e, bk=bk, aT_c=aT_c: e.matmul(bk[:, 128:192], c_ustrict, aT_c, start=True, stop=True), [("TE", 1)])
            k.wait("vector", "TE")
            k.op("vector", lambda e, bk=bk, c=c: e.tensor_copy(s_cAT[:, c * 64:(c + 1) * 64], bk[:, 0:64]))
            k.op("vector", lambda e, bk=bk, c=c: e.tensor_copy(s_cumA[:, c * 64:(c + 1) * 64], bk[:, 64:128]))
            k.op("vector", lambda e, bk=bk, c=c: e.tensor_copy(s_revT[:, c * 64:(c + 1) * 64], bk[:, 128:192]), [("VE", 1)])
        k.op("vector", lambda e: e.tensor_copy(s_chA, s_cumA[:, 63::64]))
        vsep()
        k.op("vector", lambda e: e.memset(s_off[:, 0:1], 0.0))
        vsep()
        for c in range(1, NCH):
            k.op("vector", lambda e, c=c: TT(e, s_off[:, c:c + 1], s_off[:, c - 1:c], s_chA[:, c - 1:c], OP.add))
            vsep()
        k.mark("vector", "VE")
        r_glob = R(0, 1024, pn=64)
        r_cde = R(1024, 1024, pn=64)
        k.op("vector", lambda e: TT(e, r_glob.rearrange("p (a b) -> p a b", b=64),
                                    s_cumA.rearrange("p (a b) -> p a b", b=64),
                                    s_off.unsqueeze(2).broadcast_to([64, 16, 64]), OP.add), [("VE", 1)])
        k.wait("scalar", "VE")
        k.op("scalar", lambda e: e.activation(stg[0:64, 0:16], s_chA, AF.Exp), [("SE", 1)])
        k.wait("vector", "SE")
        k.op("vector", lambda e: e.tensor_copy(
            r_cde.rearrange("p (a b) -> p a b", b=64),
            stg[0:64, 0:16].unsqueeze(2).broadcast_to([64, 16, 64])), [("VE", 1)])
        k.wait("gpsimd", "VE")
        k.store(cflat_d.rearrange("c (h t) -> h c t", h=64), s_cumA.rearrange("h (c t) -> h c t", c=16))
        k.store(gflat_d.rearrange("c (h t) -> h c t", h=64), r_glob.rearrange("h (c t) -> h c t", c=16))
        k.store(cdex_d.rearrange("c (h p) -> h c p", h=64), r_cde.rearrange("h (c p) -> h c p", c=16))
        k.wait("sync", "GD")

    if stage >= 7:
        phase_guard()
        r_rep = lambda par: R(4096 * par, 4096)
        r_ssL = R(8192, 4096, pn=64)
        r_cdec = R(12288, 4096)
        r_Xc = R(16384, 4096, pn=64)
        r_Xs = R(20480, 4096, pn=64)
        r_Yst = R(24576, 4096, pn=64)
        r_cdr = R(28672, 4096)

        k.op("vector", lambda e: e.memset(state_t[:], 0.0), [("VE", 1)])
        ve_state_done = k.cnt["VE"]
        rep_se = [0, 0]
        rep_ve = [0, 0]
        te_y_done = 0
        te_k_done = 0
        ve_evac = {0: 0, 1: 0}
        ve_xsd = 0
        se_ds = 0
        for c in range(SSD_LIMIT):
            par = c % 2
            k.wait("sync", "SE", rep_se[par])
            k.wait("sync", "VE", rep_ve[par])
            k.load(r_rep(par), cflat_d[c, :].unsqueeze(0).broadcast_to([128, 4096]), sem=LR[par])
            k.wait("sync", "TE", te_k_done)
            k.load(r_Xc, X_d[c * 64:(c + 1) * 64, :], sem="LX")
            xc_ld = k.cnt["LX"]
            k.wait("sync", "VE", ve_state_done)
            k.load(r_cdr, cdex_d[c, :].unsqueeze(0).broadcast_to([128, 4096]), sem="LCD")
            cdr_ld = k.cnt["LCD"]

            k.wait("scalar", LR[par])
            k.wait("scalar", "TE", te_y_done)
            k.wait("scalar", "VE")
            k.op("scalar", lambda e, par=par: e.activation(r_cdec, r_rep(par), AF.Exp), [("SE", 1)])
            rep_se[par] = k.cnt["SE"]
            k.wait("scalar", "VE", ve_xsd)
            k.op("scalar", lambda e, c=c: e.activation(s_dsT, s_revT[:, c * 64:(c + 1) * 64], AF.Exp), [("SE", 1)])
            se_ds = k.cnt["SE"]

            CT_c = s_CT[:, c * 64:(c + 1) * 64]
            BT_c = s_BT[:, c * 64:(c + 1) * 64]
            k.wait("vector", "SE", rep_se[par])
            k.op("vector", lambda e, CT_c=CT_c: TT(
                e, r_cdec.rearrange("p (h t) -> p h t", h=64),
                CT_c.unsqueeze(1).broadcast_to([128, 64, 64]),
                r_cdec.rearrange("p (h t) -> p h t", h=64), OP.mult), [("VE", 1)])

            k.wait("tensor", "VE")
            k.op("tensor", lambda e, BT_c=BT_c, CT_c=CT_c:
                 e.matmul(bank(6, pn=64, cn=64), BT_c, CT_c, start=True, stop=True), [("TE", 1)])
            te_cb = k.cnt["TE"]

            k.wait("vector", LR[par])
            k.wait("vector", "TE", te_y_done)
            cAT_c = s_cAT[:, c * 64:(c + 1) * 64]
            k.op("vector", lambda e, par=par, cAT_c=cAT_c: TT(
                e, r_ssL.rearrange("p (h t) -> p h t", h=64),
                r_rep(par)[0:64, :].rearrange("p (h t) -> p h t", h=64),
                cAT_c.unsqueeze(2).broadcast_to([64, 64, 64]), OP.subtract))
            k.op("vector", lambda e: e.tensor_scalar_min(r_ssL, r_ssL, 0.0), [("VE", 1)])
            rep_ve[par] = k.cnt["VE"]
            k.wait("scalar", "VE", rep_ve[par])
            k.op("scalar", lambda e: e.activation(r_ssL, r_ssL, AF.Exp), [("SE", 1)])
            se_L = k.cnt["SE"]
            k.wait("vector", "TE", te_cb)
            k.op("vector", lambda e: TT(e, s_cbm, bank(6, pn=64, cn=64), c_mask64, OP.mult))
            vsep()
            k.wait("vector", "SE", se_L)
            k.op("vector", lambda e: TT(
                e, r_ssL.rearrange("p (h t) -> p h t", h=64),
                r_ssL.rearrange("p (h t) -> p h t", h=64),
                s_cbm.unsqueeze(1).broadcast_to([64, 64, 64]), OP.mult), [("VE", 1)])
            if DEBUG and c == 8:
                k.wait("gpsimd", "VE")
                k.store(dbg_d[0:64, 4096:8192], r_ssL)
            k.wait("vector", "LX", xc_ld)
            dt2_c = s_dt2c[:, c * 64:(c + 1) * 64]
            k.op("vector", lambda e, dt2_c=dt2_c: TT(
                e, r_Xs.rearrange("p (h w) -> p h w", h=64),
                r_Xc.rearrange("p (h w) -> p h w", h=64),
                dt2_c.unsqueeze(2).broadcast_to([64, 64, 64]), OP.mult), [("VE", 1)])
            ve_mtxs = k.cnt["VE"]

            k.wait("tensor", "VE", ve_mtxs)
            k.wait("tensor", "VE", ve_state_done)
            for half in range(2):
                pb = pBig0 if half == 0 else pBig1
                k.wait("tensor", "VE", ve_evac[half])
                for j in range(32):
                    hh = half * 32 + j
                    k.op("tensor", lambda e, pb=pb, j=j, hh=hh:
                         e.matmul(pb[0:64, j * 64:(j + 1) * 64],
                                  r_ssL[:, hh * 64:(hh + 1) * 64],
                                  r_Xs[:, hh * 64:(hh + 1) * 64],
                                  start=True, stop=False))
                    k.op("tensor", lambda e, pb=pb, j=j, hh=hh:
                         e.matmul(pb[0:64, j * 64:(j + 1) * 64],
                                  r_cdec[:, hh * 64:(hh + 1) * 64],
                                  state_t[:, hh * 64:(hh + 1) * 64],
                                  start=False, stop=True),
                         [("TE", 1)] if j == 31 else [])
                te_half = k.cnt["TE"]
                k.wait("vector", "TE", te_half)
                k.wait("vector", "GD")
                k.op("vector", lambda e, pb=pb, half=half:
                     e.tensor_copy(r_Yst[:, half * 2048:(half + 1) * 2048], pb[0:64, :]), [("VE", 1)])
                ve_evac[half] = k.cnt["VE"]
            te_y_done = k.cnt["TE"]
            k.wait("gpsimd", "VE", ve_evac[1])
            k.store(Y_d[c * 64:(c + 1) * 64, :], r_Yst)

            k.wait("vector", "SE", se_ds)
            k.op("vector", lambda e: TT(
                e, r_Xs.rearrange("p (h w) -> p h w", h=64),
                r_Xs.rearrange("p (h w) -> p h w", h=64),
                s_dsT.unsqueeze(2).broadcast_to([64, 64, 64]), OP.mult), [("VE", 1)])
            ve_xsd = k.cnt["VE"]

            k.wait("tensor", "VE", ve_xsd)
            Btm_c = s_Btm[:, c * 128:(c + 1) * 128]
            for half in range(2):
                pb = pBig0 if half == 0 else pBig1
                k.wait("tensor", "VE", ve_evac[half])
                for pc in range(4):
                    k.op("tensor", lambda e, pb=pb, half=half, pc=pc, Btm_c=Btm_c:
                         e.matmul(pb[:, pc * 512:(pc + 1) * 512], Btm_c,
                                  r_Xs[:, half * 2048 + pc * 512:half * 2048 + (pc + 1) * 512],
                                  start=True, stop=True),
                         [("TE", 1)] if (half == 1 and pc == 3) else [])
            te_k_done = k.cnt["TE"]

            k.wait("vector", "LCD", cdr_ld)
            k.wait("vector", "TE", te_k_done)
            k.op("vector", lambda e: TT(e, state_t[:], state_t[:], r_cdr, OP.mult))
            k.op("vector", lambda e: TT(e, state_t[:, 0:2048], state_t[:, 0:2048], pBig0[:, :], OP.add))
            k.op("vector", lambda e: TT(e, state_t[:, 2048:4096], state_t[:, 2048:4096], pBig1[:, :], OP.add), [("VE", 1)])
            ve_state_done = k.cnt["VE"]
        if DEBUG:
            k.wait("gpsimd", "VE", ve_state_done)
            k.store(dbg_d[:, 0:4096], state_t[:])
        k.wait("sync", "GD")

    if stage >= 8:
        k.wait("gpsimd", "VE")
        k.store(ag_in_d, state_t[:])
        k.wait("gpsimd", "GD")
        k.op("gpsimd", lambda e: e.collective_compute(
            "AllGather", OP.bypass,
            replica_groups=[[0, 1], [2, 3], [4, 5], [6, 7]],
            ins=[ag_in_d.opt()], outs=[ag_out_d.opt()]), [("CC", 1)])
        k.wait("sync", "CC")
        k.load(state_t[:], ag_out_d[0:128, :])
        k.wait("vector", "LD")
        k.op("vector", lambda e: e.tensor_scalar_mul(state_t[:], state_t[:], c_sel), [("VE", 1)])
        ve_sin = k.cnt["VE"]

    if stage >= 9:
        r_Yh = R(0, 2048)
        r_Xh = R(2048, 2048)
        r_szh = R(4096, 2048)
        r_Dex = R(6144, 4096)
        r_rg = lambda par: R(10240 + 4096 * par, 4096)
        r_cg = R(18432, 4096)
        r_Ys = R(22528, 2048)
        r_Tst = R(24576, 2048)
        phase_guard()
        k.load(r_Dex, dvec.unsqueeze(0).broadcast_to([128, 4096]), sem="LA0")
        rg_se = [0, 0]
        cg_ve = 0
        te_corr = 0
        ys_ve = 0
        prev_tr_ve = 0
        for tt in range(NT):
            for half in range(2):
                hp0 = half * 2048
                k.wait("sync", "VE", ys_ve)
                k.load(r_Yh, Y_d[tt * 128:(tt + 1) * 128, hp0:hp0 + 2048], sem="LA0")
                k.load(r_Xh, X_d[tt * 128:(tt + 1) * 128, hp0:hp0 + 2048], sem="LA0")
                k.load(r_szh, sz_d[tt * 128:(tt + 1) * 128, hp0:hp0 + 2048], sem="LA0")
                ld_yxs = k.cnt["LA0"]
                for ci in range(2):
                    c = tt * 2 + ci
                    par = c % 2
                    k.wait("sync", "SE", rg_se[par])
                    k.load(r_rg(par), gflat_d[c, :].unsqueeze(0).broadcast_to([128, 4096]), sem=LR[par])
                    k.wait("scalar", LR[par])
                    k.wait("scalar", "VE", cg_ve)
                    k.wait("scalar", "TE", te_corr)
                    k.op("scalar", lambda e, par=par: e.activation(r_cg, r_rg(par), AF.Exp), [("SE", 1)])
                    rg_se[par] = k.cnt["SE"]
                    k.wait("vector", "SE", rg_se[par])
                    CT_c = s_CT[:, c * 64:(c + 1) * 64]
                    k.op("vector", lambda e, CT_c=CT_c: TT(
                        e, r_cg.rearrange("p (h t) -> p h t", h=64),
                        CT_c.unsqueeze(1).broadcast_to([128, 64, 64]),
                        r_cg.rearrange("p (h t) -> p h t", h=64), OP.mult), [("VE", 1)])
                    cg_ve = k.cnt["VE"]
                    k.wait("tensor", "VE", cg_ve)
                    if tt == 0 and half == 0 and ci == 0:
                        k.wait("tensor", "VE", ve_sin)
                    k.wait("tensor", "VE", ys_ve)
                    k.wait("tensor", "VE", prev_tr_ve)
                    pb = pBig0 if ci == 0 else pBig1
                    pofs = 0 if ci == 0 else 64
                    tp = None if ci == 0 else (0, 64)
                    for j in range(32):
                        hh = half * 32 + j
                        k.op("tensor", lambda e, pb=pb, j=j, hh=hh, pofs=pofs, tp=tp:
                             e.matmul(pb[pofs:pofs + 64, j * 64:(j + 1) * 64],
                                      r_cg[:, hh * 64:(hh + 1) * 64],
                                      state_t[:, hh * 64:(hh + 1) * 64],
                                      start=True, stop=True,
                                      tile_position=tp),
                             [("TE", 1)] if j == 31 else [])
                    te_corr = k.cnt["TE"]
                k.wait("vector", "LA0", ld_yxs)
                k.wait("vector", "TE", te_corr)
                for ci in range(2):
                    pb = pBig0 if ci == 0 else pBig1
                    pofs = 0 if ci == 0 else 64
                    rows = slice(ci * 64, ci * 64 + 64)
                    k.op("vector", lambda e, rows=rows, hp0=hp0: TT(
                        e, r_Ys[rows, :], r_Xh[rows, :], r_Dex[rows, hp0:hp0 + 2048], OP.mult))
                    k.op("vector", lambda e, rows=rows: TT(e, r_Ys[rows, :], r_Ys[rows, :], r_Yh[rows, :], OP.add))
                    k.op("vector", lambda e, rows=rows, pb=pb, pofs=pofs: TT(
                        e, r_Ys[rows, :], r_Ys[rows, :], pb[pofs:pofs + 64, :], OP.add))
                    k.op("vector", lambda e, rows=rows: TT(e, r_Ys[rows, :], r_Ys[rows, :], r_szh[rows, :], OP.mult), [("VE", 1)])
                ys_ve = k.cnt["VE"]
                k.wait("tensor", "VE", ys_ve)
                grp_ve = {0: 0, 1: 0}
                for g in range(4):
                    bk = bank(6 + g % 2)
                    k.wait("tensor", "VE", grp_ve[g % 2])
                    for j in range(4):
                        di = g * 4 + j
                        k.op("tensor", lambda e, bk=bk, j=j, di=di:
                             e.transpose(bk[:, j * 128:(j + 1) * 128], r_Ys[:, di * 128:(di + 1) * 128], c_ident), [("TE", 1)])
                    k.wait("vector", "TE")
                    k.wait("vector", "GD")
                    k.op("vector", lambda e, bk=bk, g=g: e.tensor_copy(r_Tst[:, g * 512:(g + 1) * 512], bk), [("VE", 1)])
                    grp_ve[g % 2] = k.cnt["VE"]
                ve_tr = k.cnt["VE"]
                prev_tr_ve = ve_tr
                k.wait("gpsimd", "VE", ve_tr)
                k.store(YsT_d.rearrange("(dt p) t -> p dt t", p=128)[:, half * 16:half * 16 + 16, tt * 128:(tt + 1) * 128],
                        r_Tst[:].rearrange("p (dt c) -> p dt c", c=128))
        k.wait("sync", "GD")

    if stage >= 10:
        phase_guard()
        r_ysth = R(0, 16384)
        r_wsl = lambda par: R(16384 + 8192 * par, 8192)
        wsl_te = [0, 0]
        xm_ve = {0: 0, 1: 0, 2: 0, 3: 0}
        for th in range(2):
            if th > 0:
                k.wait("sync", "TE")
            k.load(r_ysth.rearrange("p (dt t) -> p dt t", t=512),
                   YsT_d.rearrange("(dt p) t -> p dt t", p=128)[:, :, th * 512:(th + 1) * 512],
                   sem="LR0")
            for oc in range(8):
                par = oc % 2
                k.wait("sync", "TE", wsl_te[par])
                k.load(r_wsl(par).rearrange("p (dt o) -> p dt o", o=256),
                       woutT.rearrange("(dt p) o -> p dt o", p=128)[:, :, oc * 256:(oc + 1) * 256],
                       sem=LW[par])
                k.wait("tensor", LW[par])
                if oc == 0:
                    k.wait("tensor", "LR0")
                for tq in range(4):
                    tt = th * 4 + tq
                    bk = bank(tq, c0=(oc % 2) * 256, cn=256)
                    if oc % 2 == 0:
                        k.wait("tensor", "VE", xm_ve[tq])
                    for di in range(NDI):
                        k.op("tensor", lambda e, bk=bk, di=di, tq=tq, par=par:
                             e.matmul(bk, r_ysth[:, di * 512 + tq * 128:di * 512 + (tq + 1) * 128],
                                      r_wsl(par)[:, di * 256:(di + 1) * 256],
                                      start=(di == 0), stop=(di == NDI - 1)),
                             [("TE", 1)] if (tq == 3 and di == NDI - 1) else [])
                wsl_te[par] = k.cnt["TE"]
                if oc % 2 == 1:
                    te_b = k.cnt["TE"]
                    for tq in range(4):
                        tt = th * 4 + tq
                        k.wait("sync", "VE", xm_ve[tq])
                        k.load(stg[:, tq * 512:tq * 512 + 512],
                               xin[tt * 128:(tt + 1) * 128, (oc - 1) * 256:(oc + 1) * 256], sem="LA0")
                        ld_x = k.cnt["LA0"]
                        k.wait("vector", "TE", te_b)
                        k.wait("vector", "LA0", ld_x)
                        k.wait("vector", "GD")
                        k.op("vector", lambda e, tq=tq: TT(e, stg2[:, tq * 512:tq * 512 + 512],
                                                           bank(tq), stg[:, tq * 512:tq * 512 + 512], OP.add), [("VE", 1)])
                        xm_ve[tq] = k.cnt["VE"]
                        k.wait("gpsimd", "VE", xm_ve[tq])
                        k.store(xmid_d[tt * 128:(tt + 1) * 128, (oc - 1) * 256:(oc + 1) * 256],
                                stg2[:, tq * 512:tq * 512 + 512])
        k.wait("sync", "GD")

    if stage >= 11:
        phase_guard()
        k.op("vector", lambda e: e.memset(c_gmx, 0.0), [("VE", 1)])
        for tt in range(NT):
            par = tt % 2
            k.wait("sync", "SE")
            k.load(r_x(par), xmid_d[tt * 128:(tt + 1) * 128, :], sem="LA0")
            k.wait("vector", "LA0")
            xt = r_x(par)
            k.op("vector", lambda e, xt=xt: e.reduce_sum(c_t0, xt, axis=AX.X))
            vsep()
            k.op("vector", lambda e, tt=tt: e.tensor_scalar_mul(c_m2(tt), c_t0, 1.0 / D))
            k.op("vector", lambda e, xt=xt: e.reduce_max(c_mx, xt, axis=AX.X))
            k.op("vector", lambda e, xt=xt: e.tensor_reduce(c_mn, xt, AX.X, OP.min))
            vsep()
            k.op("vector", lambda e, xt=xt, tt=tt, par=par: e.tensor_scalar_sub(r_dd(par), xt, c_m2(tt)))
            k.op("vector", lambda e, par=par: TT(e, r_ts(par), r_dd(par), r_dd(par), OP.mult))
            k.op("vector", lambda e, par=par: e.reduce_sum(c_vv, r_ts(par), axis=AX.X))
            vsep()
            k.op("vector", lambda e: e.tensor_scalar_mul(c_vv, c_vv, 1.0 / D), [("VE", 1)])
            k.wait("scalar", "VE")
            k.op("scalar", lambda e: e.activation(c_t1, c_vv, AF.Sqrt, bias=c_eps), [("SE", 1)])
            k.wait("vector", "SE")
            k.op("vector", lambda e, tt=tt: e.reciprocal(c_rs2(tt), c_t1))
            vsep()
            k.op("vector", lambda e, tt=tt: TT(e, c_mx, c_mx, c_m2(tt), OP.subtract))
            vsep()
            k.op("vector", lambda e, tt=tt: TT(e, c_mn, c_m2(tt), c_mn, OP.subtract))
            vsep()
            k.op("vector", lambda e: TT(e, c_mx, c_mx, c_mn, OP.max))
            vsep()
            k.op("vector", lambda e, tt=tt: TT(e, c_mx, c_mx, c_rs2(tt), OP.mult))
            vsep()
            k.op("vector", lambda e: TT(e, c_gmx, c_gmx, c_mx, OP.max), [("VE", 1)])
        k.wait("gpsimd", "VE")
        k.store(smax_d[0:128], c_gmx)
        k.wait("sync", "GD")
        k.load(stg[0:1, 0:128], smax_d[0:128].unsqueeze(0))
        k.wait("vector", "LD")
        k.op("vector", lambda e: e.reduce_max(stg[0:1, 128:129], stg[0:1, 0:128], axis=AX.X))
        vsep()
        k.op("vector", lambda e: e.tensor_copy(stg[0:1, 136:144], stg[0:1, 128:129].broadcast_to([1, 8])), [("VE", 1)])
        k.wait("gpsimd", "VE")
        k.store(mx_in_d, stg[0:1, 136:144])
        k.wait("gpsimd", "GD")
        k.op("gpsimd", lambda e: e.collective_compute(
            "AllReduce", OP.max, replica_groups=[list(range(CC_WORLD))],
            ins=[mx_in_d.opt()], outs=[mx_out_d.opt()]), [("CC", 1)])
        k.wait("sync", "CC")
        k.load(stg[0:1, 144:152], mx_out_d)
        k.wait("vector", "LD")
        k.op("vector", lambda e: e.tensor_scalar(stg[0:1, 152:153], stg[0:1, 144:145], 1.0 / 448.0, 1e-7, OP.mult, OP.max))
        vsep()
        k.op("vector", lambda e: e.reciprocal(stg[0:1, 153:154], stg[0:1, 152:153]), [("VE", 1)])
        k.wait("gpsimd", "VE")
        k.store(sc_d[0:1, 0:1], stg[0:1, 153:154])
        k.wait("sync", "GD")
        k.load(c_invsc, sc_d[0:1, 0:1].broadcast_to([128, 1]))
        k.wait("vector", "LD")
        k.mark("vector", "VE")

    if stage >= 12:
        phase_guard()
        xq_area = R(0, 8192)
        s_xq = xq_area.bitcast(BF16)
        r_q = R(8192, 2048)
        r_xa = R(10240, 2048)
        r_da = R(12288, 2048)
        ve_q = 0
        for tt in range(NT):
            k.wait("sync", "VE", ve_q)
            k.load(r_xa, xmid_d[tt * 128:(tt + 1) * 128, :], sem="LA0")
            k.wait("vector", "LA0")
            k.op("vector", lambda e, tt=tt: e.tensor_scalar(r_da, r_xa, c_m2(tt), c_rs2(tt), OP.subtract, OP.mult), [("VE", 1)])
            k.wait("tensor", "VE")
            grp_ve = {0: 0, 1: 0}
            for g in range(4):
                bk = bank(0 + g % 2)
                k.wait("tensor", "VE", grp_ve[g % 2])
                for j in range(4):
                    di = g * 4 + j
                    k.op("tensor", lambda e, bk=bk, j=j, di=di:
                         e.transpose(bk[:, j * 128:(j + 1) * 128], r_da[:, di * 128:(di + 1) * 128], c_ident), [("TE", 1)])
                k.wait("vector", "TE")
                k.op("vector", lambda e, bk=bk, g=g: e.tensor_copy(r_q[:, g * 512:(g + 1) * 512], bk), [("VE", 1)])
                grp_ve[g % 2] = k.cnt["VE"]
            a = r_q
            ai = a.bitcast(I32)
            sb_ = stg[:, 0:2048]
            sbi = sb_.bitcast(I32)
            s2 = stg2[:, 0:2048]
            s2i = s2.bitcast(I32)
            k.op("vector", lambda e, a=a: e.tensor_scalar_mul(a, a, c_invsc))
            k.op("vector", lambda e, ai=ai, sbi=sbi: e.tensor_single_scalar(sbi, ai, 0x7F800000, OP.bitwise_and))
            k.op("vector", lambda e, sbi=sbi: e.tensor_single_scalar(sbi, sbi, 3 << 23, OP.subtract))
            k.op("vector", lambda e, sbi=sbi: e.tensor_single_scalar(sbi, sbi, 118 << 23, OP.max))
            k.op("vector", lambda e, s2i=s2i, sbi=sbi: e.tensor_scalar(s2i, sbi, -1, 0x7F000000, OP.mult, OP.add))
            k.op("vector", lambda e, a=a, s2=s2: TT(e, a, a, s2, OP.mult))
            k.op("vector", lambda e, a=a: e.tensor_scalar(a, a, C15, -C15, OP.add, OP.add))
            k.op("vector", lambda e, a=a, sb_=sb_: TT(e, a, a, sb_, OP.mult))
            k.op("vector", lambda e, tt=tt, a=a: e.tensor_copy(
                s_xq.rearrange("p (dt c) -> p dt c", c=1024)[:, :, tt * 128:(tt + 1) * 128],
                a.rearrange("p (dt c) -> p dt c", c=128)), [("VE", 1)])
            ve_q = k.cnt["VE"]
        if DEBUG:
            k.wait("gpsimd", "VE", ve_q)
            k.store(dbg_d[:, 4096:6144], xq_area[:, 0:2048])

    if stage >= 13:
        phase_guard()
        r_fs = lambda par: R(14336 + 4096 * par, 4096)
        r_fb = R(22528, 2048)
        k.load(r_fb, fbias.unsqueeze(0).broadcast_to([128, D]), sem="LC")
        fb_ld = k.cnt["LC"]
        fs_te = [0, 0]
        out_ve = {0: 0, 1: 0}
        ve_q_all = k.cnt["VE"]
        k.wait("tensor", "VE", ve_q_all)
        for opi in range(4):
            par = opi % 2
            k.wait("gpsimd", "TE", fs_te[par])
            fslab = r_fs(par).bitcast(BF16)
            k.op("gpsimd", lambda e, fslab=fslab, opi=opi: e.dma_start(
                out=fslab[:, 0:8192].rearrange("p (dt o) -> p dt o", o=512),
                in_=fwT.rearrange("(dt p) o -> p dt o", p=128)[:, :, opi * 512:(opi + 1) * 512]), [("GF", 16)])
            gf_ld = k.cnt["GF"]
            for tt in range(NT):
                bk = bank(0 + tt % 2)
                k.wait("tensor", "GF", gf_ld)
                k.wait("tensor", "VE", out_ve[tt % 2])
                for di in range(ND):
                    k.op("tensor", lambda e, bk=bk, di=di, tt=tt, fslab=fslab:
                         e.matmul(bk, s_xq.rearrange("p (dt c) -> p dt c", c=1024)[:, di, tt * 128:(tt + 1) * 128],
                                  fslab[:, di * 512:(di + 1) * 512],
                                  start=(di == 0), stop=(di == ND - 1)),
                         [("TE", 1)] if di == ND - 1 else [])
                te_f = k.cnt["TE"]
                xm_t = stg[:, 512 + (tt % 2) * 512:512 + (tt % 2) * 512 + 512]
                k.wait("sync", "VE", out_ve[tt % 2])
                k.load(xm_t, xmid_d[tt * 128:(tt + 1) * 128, opi * 512:(opi + 1) * 512], sem=("LA0" if tt % 2 == 0 else "LA1"))
                ld_xm = k.cnt["LA0" if tt % 2 == 0 else "LA1"]
                k.wait("vector", "TE", te_f)
                k.wait("vector", "LA0" if tt % 2 == 0 else "LA1", ld_xm)
                k.wait("vector", "LC", fb_ld)
                k.wait("vector", "GD")
                ot_t = stg2[:, (tt % 2) * 512:(tt % 2) * 512 + 512]
                k.op("vector", lambda e, bk=bk, xm_t=xm_t, ot_t=ot_t: e.scalar_tensor_tensor(
                    ot_t, bk, c_ws, xm_t, OP.mult, OP.add))
                k.op("vector", lambda e, opi=opi, ot_t=ot_t: TT(e, ot_t, ot_t, r_fb[:, opi * 512:(opi + 1) * 512], OP.add), [("VE", 1)])
                out_ve[tt % 2] = k.cnt["VE"]
                k.wait("gpsimd", "VE", out_ve[tt % 2])
                k.store(out[tt * 128:(tt + 1) * 128, opi * 512:(opi + 1) * 512], ot_t)
            fs_te[par] = k.cnt["TE"]

    k.wait("gpsimd", "GD")
    k.wait("gpsimd", "GF")

    with es:
        with nc.Block() as block:
            @block.sync
            def _(e):
                for f in k.q["sync"]:
                    f(e)

            @block.tensor
            def _(e):
                for f in k.q["tensor"]:
                    f(e)

            @block.vector
            def _(e):
                for f in k.q["vector"]:
                    f(e)

            @block.scalar
            def _(e):
                for f in k.q["scalar"]:
                    f(e)

            @block.gpsimd
            def _(e):
                for f in k.q["gpsimd"]:
                    f(e)
    return nc


def make_in_maps(inputs):
    f32 = np.float32
    x = np.ascontiguousarray(np.asarray(inputs["L0_attn_x"], f32))
    winT = np.ascontiguousarray(np.asarray(inputs["L0_attn_in_proj_w"], f32).T)
    convw = np.ascontiguousarray(np.asarray(inputs["L0_attn_conv_w"], f32))
    convb = np.ascontiguousarray(np.asarray(inputs["L0_attn_conv_b"], f32))
    wxT = np.ascontiguousarray(np.asarray(inputs["L0_attn_x_proj_w"], f32).T)
    wdtT = np.ascontiguousarray(np.asarray(inputs["L0_attn_dt_proj_w"], f32).T)
    dtb = np.ascontiguousarray(np.asarray(inputs["L0_attn_dt_proj_b"], f32))
    alog = np.ascontiguousarray(np.asarray(inputs["L0_attn_A_log"], f32))
    dvec = np.ascontiguousarray(np.repeat(np.asarray(inputs["L0_attn_D"], f32), P_))
    woutT = np.ascontiguousarray(np.asarray(inputs["L0_attn_out_proj_w"], f32).T)
    fwT = np.ascontiguousarray(np.asarray(inputs["L0_ffn_weight"], f32).T)
    fb = np.ascontiguousarray(np.asarray(inputs["L0_ffn_bias"], f32))
    wsc = np.asarray(inputs["L0_ffn_weight_scale"], f32).reshape(1, 1)
    ident = np.eye(128, dtype=f32)
    ar = np.arange(64)
    uincl = (ar[:, None] <= ar[None, :]).astype(f32)
    ustrict = (ar[:, None] > ar[None, :]).astype(f32)
    mask64 = (ar[None, :] >= ar[:, None]).astype(f32)
    maps = []
    for core in range(8):
        b, hh = core // 2, core % 2
        t0 = hh * T
        xi = np.ascontiguousarray(x[b, t0:t0 + T])
        xpr = np.ascontiguousarray(x[b, t0 - 3:t0]) if hh == 1 else np.zeros((3, D), f32)
        sel = np.full((128, 1), float(hh), f32)
        maps.append(dict(xin=xi, xprev=xpr, winT=winT, convw=convw, convb=convb,
                         wxT=wxT, wdtT=wdtT, dtbias=dtb, alog=alog, dvec=dvec,
                         woutT=woutT, fwT=fwT, fbias=fb, wsc=wsc, selc=sel,
                         ident=ident, uincl=uincl, ustrict=ustrict, mask64=mask64))
    return maps


def kernel(**inputs):
    nc = build_program(13)
    maps = make_in_maps(inputs)
    res = run_bass_kernel_spmd(nc, maps, core_ids=list(range(8)))
    outf = np.zeros((4, 2048, D), np.float32)
    for core in range(8):
        b, hh = core // 2, core % 2
        outf[b, hh * T:(hh + 1) * T] = res.results[core]["out"]
    return outf


# version 04 (final)
# speedup vs baseline: 1.0161x; 1.0102x over previous
import numpy as np
from contextlib import ExitStack

import concourse.bass as bass
import concourse.mybir as mybir
from concourse.bass_utils import run_bass_kernel_spmd

F32 = mybir.dt.float32
BF16 = mybir.dt.bfloat16
I32 = mybir.dt.int32
AF = mybir.ActivationFunctionType
OP = mybir.AluOpType
AX = mybir.AxisListType

D = 2048
DI = 4096
H = 64
P_ = 64
NS = 128
T = 1024
NCH = 16
XPD = 4416
NT = 8
ND = 16
NDI = 32

C15 = float(np.float32(1.5 * 2 ** 23))
XPROJ_OG = 8
XPROJ_J = 4
SSD_LIMIT = 16
CC_WORLD = 8
DEBUG = False


class PK:
    def __init__(self, nc):
        self.nc = nc
        self.q = {e: [] for e in ("tensor", "vector", "scalar", "gpsimd", "sync")}
        self.sems = {}
        self.cnt = {}

    def add_sem(self, name, handle):
        self.sems[name] = handle
        self.cnt[name] = 0

    def emit(self, eng, fn):
        self.q[eng].append(fn)

    def op(self, eng, fn, incs=()):
        incs = tuple(incs)
        for s, n in incs:
            self.cnt[s] += n

        def g(e, fn=fn, incs=incs, sems=self.sems):
            ins = fn(e)
            for s, n in incs:
                ins = ins.then_inc(sems[s], n)
        self.emit(eng, g)
        return {s: self.cnt[s] for s, _ in incs}

    def wait(self, eng, sem, val=None):
        v = self.cnt[sem] if val is None else val
        if v > 0:
            self.emit(eng, lambda e, s=self.sems[sem], v=v: e.wait_ge(s, v))

    def mark(self, eng, sem):
        if eng == "vector":
            return self.op(eng, lambda e: e.tensor_copy(self.junk_dst, self.junk_src), [(sem, 1)])[sem]
        if eng == "scalar":
            return self.op(eng, lambda e: e.copy(self.junk_dst3, self.junk_src), [(sem, 1)])[sem]
        if eng == "gpsimd":
            return self.op(eng, lambda e: e.memset(self.junk_dst2, 0.0), [(sem, 1)])[sem]
        return self.op(eng, lambda e: e.nop(), [(sem, 1)])[sem]

    def load(self, dst, src, sem="LD", slow=False):
        self.op("sync", lambda e, d=dst, s=src, sl=slow: e.dma_start(out=d, in_=s, allow_slow_non_contiguous=sl), [(sem, 16)])
        return self.cnt[sem]

    def store(self, dst, src):
        self.op("gpsimd", lambda e, d=dst, s=src: e.dma_start(out=d, in_=s), [("GD", 16)])
        return self.cnt["GD"]

    def gload(self, dst, src):
        self.op("gpsimd", lambda e, d=dst, s=src: e.dma_start(out=d, in_=s), [("GF", 16)])
        return self.cnt["GF"]


def build_program(stage=13):
    nc = bass.Bass("TRN2", target_bir_lowering=False, debug=False, detect_race_conditions=False)
    dt = nc.dram_tensor

    xin = dt("xin", [T, D], F32, kind="ExternalInput").ap()
    xprev = dt("xprev", [3, D], F32, kind="ExternalInput").ap()
    winT = dt("winT", [D, 8192], F32, kind="ExternalInput").ap()
    convw = dt("convw", [DI, 4], F32, kind="ExternalInput").ap()
    convb = dt("convb", [DI], F32, kind="ExternalInput").ap()
    wxT = dt("wxT", [DI, XPD], F32, kind="ExternalInput").ap()
    wdtT = dt("wdtT", [H, H], F32, kind="ExternalInput").ap()
    dtbias = dt("dtbias", [H], F32, kind="ExternalInput").ap()
    alog = dt("alog", [H], F32, kind="ExternalInput").ap()
    dvec = dt("dvec", [DI], F32, kind="ExternalInput").ap()
    woutT = dt("woutT", [DI, D], F32, kind="ExternalInput").ap()
    fwT = dt("fwT", [D, D], F32, kind="ExternalInput").ap()
    fbias = dt("fbias", [D], F32, kind="ExternalInput").ap()
    wsc = dt("wsc", [1, 1], F32, kind="ExternalInput").ap()
    selc = dt("selc", [128, 1], F32, kind="ExternalInput").ap()
    ident_i = dt("ident", [128, 128], F32, kind="ExternalInput").ap()
    uincl_i = dt("uincl", [64, 64], F32, kind="ExternalInput").ap()
    ustrict_i = dt("ustrict", [64, 64], F32, kind="ExternalInput").ap()
    mask64_i = dt("mask64", [64, 64], F32, kind="ExternalInput").ap()

    out = dt("out", [T, D], F32, kind="ExternalOutput").ap()

    def scr(name, shape):
        if DEBUG:
            return dt(name, shape, F32, kind="ExternalOutput").ap()
        return dt(name, shape, F32).ap()

    ln1T_d = scr("ln1T_d", [D, 1027])
    sz_d = scr("sz_d", [T, DI])
    X_d = scr("X_d", [T, DI])
    Y_d = scr("Y_d", [T, DI])
    YsT_d = scr("YsT_d", [DI, T])
    xmid_d = scr("xmid_d", [T, D])
    cflat_d = scr("cflat_d", [NCH, 4096])
    gflat_d = scr("gflat_d", [NCH, 4096])
    cdex_d = scr("cdex_d", [NCH, 4096])
    dbg_d = scr("dbg_d", [128, 8192]) if DEBUG else None
    smax_d = dt("smax_d", [256], F32).ap()
    ag_in_d = dt("ag_in_d", [128, 4096], F32).ap()
    ag_out_d = dt("ag_out_d", [256, 4096], F32).ap()
    mx_in_d = dt("mx_in_d", [1, 8], F32).ap()
    mx_out_d = dt("mx_out_d", [1, 8], F32).ap()
    sc_d = dt("sc_d", [1, 8], F32).ap()

    es = ExitStack()
    sbt = lambda name, shape, dty=F32: es.enter_context(nc.sbuf_tensor(name, shape, dty))
    pst = lambda name, shape: es.enter_context(nc.psum_tensor(name, shape, F32))

    REG_COLS = 33792
    reg_t = sbt("reg", [128, REG_COLS])
    consts = sbt("consts", [128, 1024])
    small = sbt("small", [128, 5632])
    smallB = sbt("smallB", [128, 4352])
    stg = sbt("stg", [128, 2048])
    stg2 = sbt("stg2", [128, 2048])
    state_t = sbt("state", [128, 4096])

    def R(off, cols, p0=0, pn=128):
        assert 0 <= off and off + cols <= REG_COLS, (off, cols)
        return reg_t[p0:p0 + pn, off:off + cols]

    c_ident = consts[:, 0:128]
    c_uincl = consts[0:64, 128:192]
    c_ustrict = consts[0:64, 192:256]
    c_mask64 = consts[0:64, 256:320]
    c_Arep = consts[:, 320:384]
    c_dtbrep = consts[:, 384:448]
    c_sel = consts[:, 448:449]
    c_eps = consts[:, 449:450]
    c_ws = consts[:, 450:451]
    c_invsc = consts[:, 452:453]
    c_convw = consts[:, 512:640]
    c_convb = consts[:, 640:672]
    c_junk = consts[:, 720:721]
    c_junksrc = consts[:, 721:722]
    c_m = lambda tt: consts[:, 672 + tt:673 + tt]
    c_rs = lambda tt: consts[:, 680 + tt:681 + tt]
    c_mx = consts[:, 688:689]
    c_mn = consts[:, 689:690]
    c_vv = consts[:, 690:691]
    c_t0 = consts[:, 691:692]
    c_t1 = consts[:, 692:693]
    c_gmx = consts[:, 693:694]
    c_m2 = lambda tt: consts[:, 700 + tt:701 + tt]
    c_rs2 = lambda tt: consts[:, 708 + tt:709 + tt]

    s_cAT = small[0:64, 0:1024]
    s_revT = small[0:64, 1024:2048]
    s_dt2c = small[0:64, 2048:3072]
    s_aTc = small[0:64, 3072:4096]
    s_cumA = small[0:64, 4096:5120]
    s_dsT = small[0:64, 5120:5184]
    s_cbm = small[0:64, 5184:5248]
    s_wdt = small[0:64, 5248:5312]

    s_BT = smallB[:, 0:1024]
    s_CT = smallB[:, 1024:2048]
    s_Btm = smallB[0:64, 2048:4096]
    s_chA = smallB[0:64, 4096:4112]
    s_off = smallB[0:64, 4112:4128]

    pBig0 = pst("pBig0", [128, 2048])
    pBig1 = pst("pBig1", [128, 2048])

    def bank(i, p0=0, pn=128, c0=0, cn=512):
        t = pBig0 if i < 4 else pBig1
        return t[p0:p0 + pn, (i % 4) * 512 + c0:(i % 4) * 512 + c0 + cn]

    k = PK(nc)
    k.junk_dst = consts[:, 720:721]
    k.junk_src = consts[0:128, 0:1]
    k.junk_dst2 = consts[:, 722:723]
    k.junk_dst3 = consts[:, 723:724]
    for s in ("LD", "LC", "LA0", "LA1", "LX", "LCD", "LW0", "LW1", "LR0", "LR1", "GD", "GF", "CC", "TE", "VE", "SE"):
        k.add_sem(s, es.enter_context(nc.semaphore(s)))
    LW = ("LW0", "LW1")
    LR = ("LR0", "LR1")

    def TT(e, o, a, b, op):
        return e.tensor_tensor(o, a, b, op)

    def vsep():
        k.op("vector", lambda e: e.tensor_copy(c_junk, c_ident[:, 0:1]))

    def phase_guard():
        k.wait("sync", "TE")
        k.wait("sync", "VE")
        k.wait("sync", "SE")

    def ssep():
        k.op("scalar", lambda e: e.copy(consts[:, 723:724], c_ident[:, 0:1]))

    k.load(c_ident, ident_i, sem="LC")
    k.load(c_uincl, uincl_i, sem="LC")
    k.load(c_ustrict, ustrict_i, sem="LC")
    k.load(c_mask64, mask64_i, sem="LC")
    k.load(c_Arep, alog.unsqueeze(0).broadcast_to([128, 64]), sem="LC")
    k.load(c_dtbrep, dtbias.unsqueeze(0).broadcast_to([128, 64]), sem="LC")
    k.load(c_sel, selc, sem="LC")
    k.load(c_ws, wsc.broadcast_to([128, 1]), sem="LC")
    k.load(c_convw.rearrange("p (o k) -> p o k", k=4), convw.rearrange("(o p) k -> p o k", p=128), sem="LC")
    k.load(c_convb, convb.rearrange("(o p) -> p o", p=128), sem="LC", slow=True)
    k.load(s_wdt, wdtT, sem="LC")
    k.wait("vector", "LC")
    k.op("vector", lambda e: e.memset(consts[:, 720:724], 0.0))
    k.op("vector", lambda e: e.memset(c_eps, 1e-5), [("VE", 1)])
    k.wait("scalar", "LC")
    k.op("scalar", lambda e: e.activation(stg[:, 1024:1088], c_Arep, AF.Exp), [("SE", 1)])
    k.wait("vector", "SE")
    k.op("vector", lambda e: e.tensor_scalar_mul(c_Arep, stg[:, 1024:1088], -1.0), [("VE", 1)])
    k.wait("tensor", "VE")
    k.wait("scalar", "VE")

    r_x = lambda par: R(0 + 2048 * par, 2048)
    r_dd = lambda par: R(4096 + 2048 * par, 2048)
    r_ts = lambda par: R(8192 + 2048 * par, 2048)

    def ln_tile(par, mcol, rscol, nrows=128):
        xt = r_x(par)[0:nrows, :]
        dtl = r_dd(par)[0:nrows, :]
        sq = r_ts(par)[0:nrows, :]
        k.op("vector", lambda e: e.reduce_sum(c_t0[0:nrows], xt, axis=AX.X))
        vsep()
        k.op("vector", lambda e: e.tensor_scalar_mul(mcol[0:nrows], c_t0[0:nrows], 1.0 / D))
        vsep()
        k.op("vector", lambda e: e.tensor_scalar_sub(dtl, xt, mcol[0:nrows]))
        k.op("vector", lambda e: TT(e, sq, dtl, dtl, OP.mult))
        k.op("vector", lambda e: e.reduce_sum(c_vv[0:nrows], sq, axis=AX.X))
        vsep()
        k.op("vector", lambda e: e.tensor_scalar_mul(c_vv[0:nrows], c_vv[0:nrows], 1.0 / D), [("VE", 1)])
        k.wait("scalar", "VE")
        k.op("scalar", lambda e: e.activation(c_t1[0:nrows], c_vv[0:nrows], AF.Sqrt, bias=c_eps[0:nrows]), [("SE", 1)])
        k.wait("vector", "SE")
        k.op("vector", lambda e: e.reciprocal(rscol[0:nrows], c_t1[0:nrows]))
        vsep()
        k.op("vector", lambda e: e.tensor_scalar_mul(dtl, dtl, rscol[0:nrows]), [("VE", 1)])
        return k.cnt["VE"]

    if stage >= 1:
        for tt in range(NT):
            par = tt % 2
            k.load(r_x(par), xin[tt * 128:(tt + 1) * 128, :], sem="LA0")
            k.wait("vector", "LA0")
            ve_ln = ln_tile(par, c_m(tt), c_rs(tt))
            k.wait("tensor", "VE", ve_ln)
            grp_ve = {0: 0, 1: 0}
            for g in range(4):
                bk = bank(g % 2)
                k.wait("tensor", "VE", grp_ve[g % 2])
                for j in range(4):
                    di = g * 4 + j
                    k.op("tensor", lambda e, bk=bk, j=j, di=di, par=par:
                         e.transpose(bk[:, j * 128:(j + 1) * 128],
                                     r_dd(par)[:, di * 128:(di + 1) * 128], c_ident), [("TE", 1)])
                k.wait("vector", "TE")
                k.op("vector", lambda e, bk=bk, g=g, par=par:
                     e.tensor_copy(r_ts(par)[:, g * 512:(g + 1) * 512], bk), [("VE", 1)])
                grp_ve[g % 2] = k.cnt["VE"]
            k.wait("gpsimd", "VE")
            k.store(ln1T_d.rearrange("(dt p) t -> p dt t", p=128)[:, :, 3 + tt * 128:3 + (tt + 1) * 128],
                    r_ts(par)[:].rearrange("p (dt c) -> p dt c", c=128))
            k.wait("sync", "GD")
        k.load(r_x(0)[0:3, :], xprev, sem="LA0")
        k.wait("vector", "LA0")
        ve_ln = ln_tile(0, c_t0, c_t1, nrows=3)
        k.wait("tensor", "VE", ve_ln)
        grp_ve = {0: 0, 1: 0}
        for g in range(4):
            bk = bank(g % 2)
            k.wait("tensor", "VE", grp_ve[g % 2])
            for j in range(4):
                di = g * 4 + j
                k.op("tensor", lambda e, bk=bk, j=j, di=di:
                     e.transpose(bk[:, j * 128:j * 128 + 3],
                                 r_dd(0)[0:3, di * 128:(di + 1) * 128], c_ident[0:3, 0:3]), [("TE", 1)])
            k.wait("vector", "TE")
            for j in range(4):
                k.op("vector", lambda e, bk=bk, g=g, j=j:
                     e.tensor_copy(r_ts(0)[:, (g * 4 + j) * 3:(g * 4 + j) * 3 + 3], bk[:, j * 128:j * 128 + 3]))
            k.mark("vector", "VE")
            grp_ve[g % 2] = k.cnt["VE"]
        k.wait("gpsimd", "VE")
        k.store(ln1T_d.rearrange("(dt p) t -> p dt t", p=128)[:, :, 0:3],
                r_ts(0)[:, 0:48].rearrange("p (dt c) -> p dt c", c=3))
        k.wait("sync", "GD")

    if stage >= 2:
        phase_guard()
        lnb = lambda tt: R(2048 * tt, 2048)
        zslab = lambda par: R(16384 + 8192 * par, 8192)
        for tt in range(NT):
            k.load(lnb(tt), ln1T_d.rearrange("(dt p) t -> p dt t", p=128)[:, :, 3 + tt * 128:3 + (tt + 1) * 128], sem="LA0")
        ln_ld = k.cnt["LA0"]
        slab_te = [0, 0]
        zstg_se = [0, 0]
        for op in range(8):
            par = op % 2
            k.wait("sync", "TE", slab_te[par])
            k.load(zslab(par).rearrange("p (dt o) -> p dt o", o=512),
                   winT.rearrange("(dt p) o -> p dt o", p=128)[:, :, 4096 + op * 512:4096 + (op + 1) * 512],
                   sem=LW[par])
            k.wait("tensor", LW[par])
            k.wait("tensor", "LA0", ln_ld)
            for tt in range(NT):
                bpar = tt % 2
                bk = bank(2 + bpar)
                k.wait("tensor", "SE", zstg_se[bpar])
                for di in range(ND):
                    k.op("tensor", lambda e, bk=bk, di=di, tt=tt, par=par:
                         e.matmul(bk, lnb(tt)[:, di * 128:(di + 1) * 128],
                                  zslab(par)[:, di * 512:(di + 1) * 512],
                                  start=(di == 0), stop=(di == ND - 1)),
                         [("TE", 1)] if di == ND - 1 else [])
                k.wait("scalar", "TE")
                k.wait("scalar", "GD")
                st_t = (stg if bpar == 0 else stg2)[:, 0:512]
                k.op("scalar", lambda e, st_t=st_t, bk=bk: e.activation(st_t, bk, AF.Silu), [("SE", 1)])
                zstg_se[bpar] = k.cnt["SE"]
                k.wait("gpsimd", "SE", zstg_se[bpar])
                k.store(sz_d[tt * 128:(tt + 1) * 128, op * 512:(op + 1) * 512], st_t)
            slab_te[par] = k.cnt["TE"]

    r_lnq = R(0, 8240)
    r_xbq = lambda ot: R(8240 + ot * 515, 515)
    r_wins = lambda par: R(24720 + 2048 * par, 2048)
    r_wxs = lambda par: R(24720, 8192) if par else R(0, 8192)
    r_carry = R(33280, 96)
    r_dtrawT = small[0:64, 3072:4096]

    if stage >= 3:
        for q in range(2):
            phase_guard()
            k.load(r_lnq.rearrange("p (dt w) -> p dt w", w=515),
                   ln1T_d.rearrange("(dt p) t -> p dt t", p=128)[:, :, q * 512:q * 512 + 515], sem="LA0")
            lnq_ld = k.cnt["LA0"]
            wins_te = [k.cnt["TE"], k.cnt["TE"]]
            evac_ve = {0: 0, 1: 0}
            for ot in range(NDI):
                par = ot % 2
                k.wait("sync", "TE", wins_te[par])
                k.load(r_wins(par).rearrange("p (dt o) -> p dt o", o=128),
                       winT.rearrange("(dt p) o -> p dt o", p=128)[:, :, ot * 128:(ot + 1) * 128],
                       sem=LW[par])
                bkA = bank(0 + (ot % 2))
                bkC = bank(2 + (ot % 2), c0=0, cn=3)
                k.wait("tensor", LW[par])
                if ot == 0:
                    k.wait("tensor", "LA0", lnq_ld)
                k.wait("tensor", "VE", evac_ve[ot % 2])
                for di in range(ND):
                    k.op("tensor", lambda e, bkA=bkA, di=di, par=par:
                         e.matmul(bkA, r_wins(par)[:, di * 128:(di + 1) * 128],
                                  r_lnq[:, di * 515 + 3:di * 515 + 515],
                                  start=(di == 0), stop=(di == ND - 1)))
                for di in range(ND):
                    k.op("tensor", lambda e, bkC=bkC, di=di, par=par:
                         e.matmul(bkC, r_wins(par)[:, di * 128:(di + 1) * 128],
                                  r_lnq[:, di * 515:di * 515 + 3],
                                  start=(di == 0), stop=(di == ND - 1)),
                         [("TE", 1)] if di == ND - 1 else [])
                wins_te[par] = k.cnt["TE"]
                k.wait("vector", "TE")
                k.op("vector", lambda e, ot=ot, bkA=bkA: e.tensor_copy(r_xbq(ot)[:, 3:515], bkA))
                if q == 0:
                    k.op("vector", lambda e, ot=ot, bkC=bkC: e.tensor_copy(r_xbq(ot)[:, 0:3], bkC), [("VE", 1)])
                else:
                    k.op("vector", lambda e, ot=ot, bkC=bkC: e.tensor_copy(r_xbq(ot)[:, 0:3], bkC))
                    k.op("vector", lambda e, ot=ot: e.tensor_copy(
                        r_xbq(ot)[:, 0:3], r_carry[:, ot * 3:ot * 3 + 3]), [("VE", 1)])
                evac_ve[ot % 2] = k.cnt["VE"]
            if q == 0:
                k.op("vector", lambda e: e.tensor_copy(
                    r_carry[:].rearrange("p (o c) -> p o c", c=3),
                    reg_t[:, 8240:8240 + 32 * 515].rearrange("p (o w) -> p o w", w=515)[:, :, 512:515]),
                    [("VE", 1)])

            for ot in range(NDI if stage >= 4 else 0):
                cw = lambda kk, ot=ot: c_convw[:, ot * 4 + kk:ot * 4 + kk + 1]
                cb = c_convb[:, ot:ot + 1]
                acc = stg[:, 0:512]
                if ot > 0:
                    k.wait("vector", "SE")
                k.op("vector", lambda e, ot=ot, cw=cw, acc=acc:
                     e.tensor_scalar(acc, r_xbq(ot)[:, 0:512], cw(0), None, OP.mult))
                for kk in range(1, 4):
                    k.op("vector", lambda e, ot=ot, kk=kk, cw=cw, acc=acc:
                         e.scalar_tensor_tensor(acc, r_xbq(ot)[:, kk:kk + 512], cw(kk), acc, OP.mult, OP.add))
                k.mark("vector", "VE")
                k.wait("scalar", "VE")
                k.op("scalar", lambda e, ot=ot, cb=cb, acc=acc:
                     e.activation(r_xbq(ot)[:, 3:515], acc, AF.Silu, bias=cb), [("SE", 1)])
            se_conv = k.cnt["SE"]
            if stage < 5:
                continue
            k.wait("tensor", "SE", se_conv)
            xcq = r_xbq
            wxs_te = [k.cnt["TE"], k.cnt["TE"]]
            bank47_ve = 0
            for ob in range(16):
                j = ob % 2
                par = ob % 2
                k.wait("sync", "TE", wxs_te[par])
                k.load(r_wxs(par).rearrange("p (dt o) -> p dt o", o=256),
                       wxT.rearrange("(dt p) o -> p dt o", p=128)[:, :, ob * 256:(ob + 1) * 256],
                       sem=LW[par])
                k.wait("tensor", LW[par])
                if j == 0:
                    k.wait("tensor", "VE", bank47_ve)
                for tq in range(4):
                    bk = bank(4 + tq, c0=j * 256, cn=256)
                    for di in range(NDI):
                        k.op("tensor", lambda e, bk=bk, di=di, tq=tq, par=par:
                             e.matmul(bk, xcq(di)[:, 3 + tq * 128:3 + (tq + 1) * 128],
                                      r_wxs(par)[:, di * 256:(di + 1) * 256],
                                      start=(di == 0), stop=(di == NDI - 1)),
                             [("TE", 1)] if (tq == 3 and di == NDI - 1) else [])
                wxs_te[par] = k.cnt["TE"]
                if j == 1:
                    og = ob // 2
                    k.wait("vector", "TE")
                    k.wait("vector", "GD")
                    for tq in range(4):
                        k.op("vector", lambda e, tq=tq: e.tensor_copy(stg2[:, tq * 512:(tq + 1) * 512], bank(4 + tq)), [("VE", 1)])
                    bank47_ve = k.cnt["VE"]
                    k.wait("gpsimd", "VE", bank47_ve)
                    k.store(
                        X_d.rearrange("(tq p) o -> p tq o", p=128)[:, 4 * q:4 * q + 4, og * 512:(og + 1) * 512],
                        stg2[:].rearrange("p (tq o) -> p tq o", o=512))
            for oi, (ocol0, m) in (enumerate([(4096, 64), (4160, 128), (4288, 128)]) if stage >= 6 else []):
                bk = bank(0 + oi % 2, pn=m)
                k.wait("sync", "TE")
                k.load(r_wxs(0)[:, 0:NDI * m].rearrange("p (dt o) -> p dt o", o=m),
                       wxT.rearrange("(dt p) o -> p dt o", p=128)[:, :, ocol0:ocol0 + m],
                       sem="LW0")
                k.wait("tensor", "LW0")
                k.wait("tensor", "VE")
                for di in range(NDI):
                    k.op("tensor", lambda e, bk=bk, di=di, m=m:
                         e.matmul(bk, r_wxs(0)[:, di * m:(di + 1) * m],
                                  xcq(di)[:, 3:515], start=(di == 0), stop=(di == NDI - 1)),
                         [("TE", 1)] if di == NDI - 1 else [])
                k.wait("vector", "TE")
                if oi == 0:
                    dst = r_dtrawT[0:64, q * 512:(q + 1) * 512]
                elif oi == 1:
                    dst = s_BT[:, q * 512:(q + 1) * 512]
                else:
                    dst = s_CT[:, q * 512:(q + 1) * 512]
                k.op("vector", lambda e, dst=dst, bk=bk: e.tensor_copy(dst, bk), [("VE", 1)])
            k.wait("sync", "VE")
            k.wait("sync", "TE")

    if DEBUG and 3 <= stage <= 4:
        k.wait("gpsimd", "VE")
        k.wait("gpsimd", "SE")
        for oo in range(4):
            k.store(dbg_d[:, oo * 515:(oo + 1) * 515], R(8240 + oo * 515, 515))
        k.wait("sync", "GD")

    if stage >= 6:
        phase_guard()
        k.wait("tensor", "VE")
        for c in range(NCH):
            bk = bank(0 + c % 2, pn=64, cn=128)
            if c >= 2:
                k.wait("tensor", "VE")
            k.op("tensor", lambda e, bk=bk, c=c: e.transpose(bk, s_BT[:, c * 64:(c + 1) * 64], c_ident), [("TE", 1)])
            k.wait("vector", "TE")
            k.op("vector", lambda e, bk=bk, c=c: e.tensor_copy(s_Btm[:, c * 128:(c + 1) * 128], bk), [("VE", 1)])
        for c in range(NCH):
            bk = bank(2 + c % 2, pn=64, cn=64)
            if c >= 2:
                k.wait("tensor", "VE")
            k.op("tensor", lambda e, bk=bk, c=c:
                 e.matmul(bk, r_dtrawT[0:64, c * 64:(c + 1) * 64], s_wdt, start=True, stop=True), [("TE", 1)])
            k.wait("vector", "TE")
            u = stg2[0:64, 0:64]
            z_ = stg2[0:64, 64:128]
            bb = stg2[0:64, 128:192]
            srs = stg2[0:64, 192:256]
            mm_ = stg2[0:64, 256:320]
            k.op("vector", lambda e, bk=bk, u=u: TT(e, u, bk, c_dtbrep[0:64, :], OP.add), [("VE", 1)])
            k.wait("scalar", "VE")
            k.op("scalar", lambda e, u=u, z_=z_: e.activation(z_, u, AF.Exp))
            k.op("scalar", lambda e, z_=z_, bb=bb: e.activation(bb, z_, AF.Ln, bias=1.0), [("SE", 1)])
            k.wait("vector", "SE")
            k.op("vector", lambda e, z_=z_, srs=srs: e.tensor_scalar(srs, z_, -0.2, 0.25, OP.mult, OP.add))
            for ck in (1.0 / 3, 0.5, 1.0):
                k.op("vector", lambda e, z_=z_, srs=srs: TT(e, srs, srs, z_, OP.mult))
                k.op("vector", lambda e, srs=srs, ck=ck: e.tensor_scalar(srs, srs, -1.0, ck, OP.mult, OP.add))
            k.op("vector", lambda e, z_=z_, srs=srs: TT(e, srs, srs, z_, OP.mult))
            k.op("vector", lambda e, u=u, mm_=mm_: e.tensor_single_scalar(mm_, u, -2.0, OP.is_lt))
            k.op("vector", lambda e, srs=srs, bb=bb: TT(e, srs, srs, bb, OP.subtract))
            k.op("vector", lambda e, srs=srs, mm_=mm_: TT(e, srs, srs, mm_, OP.mult))
            k.op("vector", lambda e, srs=srs, bb=bb, c=c: TT(e, s_dt2c[:, c * 64:(c + 1) * 64], srs, bb, OP.add))
            k.op("vector", lambda e, c=c: TT(e, s_aTc[:, c * 64:(c + 1) * 64],
                                             s_dt2c[:, c * 64:(c + 1) * 64], c_Arep[0:64, :], OP.mult), [("VE", 1)])
        ve_aT = k.cnt["VE"]
        k.wait("tensor", "VE", ve_aT)
        for c in range(NCH):
            aT_c = s_aTc[:, c * 64:(c + 1) * 64]
            bk = bank(0 + c % 2, pn=64)
            if c >= 2:
                k.wait("tensor", "VE")
            k.op("tensor", lambda e, bk=bk, aT_c=aT_c: e.matmul(bk[:, 0:64], c_uincl, aT_c, start=True, stop=True))
            k.op("tensor", lambda e, bk=bk, aT_c=aT_c: e.matmul(bk[:, 64:128], aT_c, c_uincl, start=True, stop=True))
            k.op("tensor", lambda e, bk=bk, aT_c=aT_c: e.matmul(bk[:, 128:192], c_ustrict, aT_c, start=True, stop=True), [("TE", 1)])
            k.wait("vector", "TE")
            k.op("vector", lambda e, bk=bk, c=c: e.tensor_copy(s_cAT[:, c * 64:(c + 1) * 64], bk[:, 0:64]))
            k.op("vector", lambda e, bk=bk, c=c: e.tensor_copy(s_cumA[:, c * 64:(c + 1) * 64], bk[:, 64:128]))
            k.op("vector", lambda e, bk=bk, c=c: e.tensor_copy(s_revT[:, c * 64:(c + 1) * 64], bk[:, 128:192]), [("VE", 1)])
        k.op("vector", lambda e: e.tensor_copy(s_chA, s_cumA[:, 63::64]))
        vsep()
        k.op("vector", lambda e: e.memset(s_off[:, 0:1], 0.0))
        vsep()
        for c in range(1, NCH):
            k.op("vector", lambda e, c=c: TT(e, s_off[:, c:c + 1], s_off[:, c - 1:c], s_chA[:, c - 1:c], OP.add))
            vsep()
        k.mark("vector", "VE")
        r_glob = R(0, 1024, pn=64)
        r_cde = R(1024, 1024, pn=64)
        k.op("vector", lambda e: TT(e, r_glob.rearrange("p (a b) -> p a b", b=64),
                                    s_cumA.rearrange("p (a b) -> p a b", b=64),
                                    s_off.unsqueeze(2).broadcast_to([64, 16, 64]), OP.add), [("VE", 1)])
        k.wait("scalar", "VE")
        k.op("scalar", lambda e: e.activation(stg[0:64, 0:16], s_chA, AF.Exp), [("SE", 1)])
        k.wait("vector", "SE")
        k.op("vector", lambda e: e.tensor_copy(
            r_cde.rearrange("p (a b) -> p a b", b=64),
            stg[0:64, 0:16].unsqueeze(2).broadcast_to([64, 16, 64])), [("VE", 1)])
        k.wait("gpsimd", "VE")
        k.store(cflat_d.rearrange("c (h t) -> h c t", h=64), s_cumA.rearrange("h (c t) -> h c t", c=16))
        k.store(gflat_d.rearrange("c (h t) -> h c t", h=64), r_glob.rearrange("h (c t) -> h c t", c=16))
        k.store(cdex_d.rearrange("c (h p) -> h c p", h=64), r_cde.rearrange("h (c p) -> h c p", c=16))
        k.wait("sync", "GD")

    if stage >= 7:
        phase_guard()
        r_rep = lambda par: R(4096 * par, 4096)
        r_ssL = R(8192, 4096, pn=64)
        r_cdec = R(12288, 4096)
        r_Xc = R(16384, 4096, pn=64)
        r_Xs = R(20480, 4096, pn=64)
        r_Yst = R(24576, 4096, pn=64)
        r_cdr = R(28672, 4096)

        k.op("vector", lambda e: e.memset(state_t[:], 0.0), [("VE", 1)])
        ve_state_done = k.cnt["VE"]
        rep_se = [0, 0]
        rep_ve = [0, 0]
        cdec_ve = 0
        te_y_done = 0
        te_k_done = 0
        ve_evac = {0: 0, 1: 0}
        ve_xsd = 0
        se_ds = 0
        for c in range(SSD_LIMIT):
            par = c % 2
            k.wait("sync", "SE", rep_se[par])
            k.wait("sync", "VE", rep_ve[par])
            k.load(r_rep(par), cflat_d[c, :].unsqueeze(0).broadcast_to([128, 4096]), sem=LR[par])
            k.wait("sync", "TE", te_k_done)
            k.load(r_Xc, X_d[c * 64:(c + 1) * 64, :], sem="LX")
            xc_ld = k.cnt["LX"]
            k.wait("sync", "VE", ve_state_done)
            k.load(r_cdr, cdex_d[c, :].unsqueeze(0).broadcast_to([128, 4096]), sem="LCD")
            cdr_ld = k.cnt["LCD"]

            k.wait("scalar", LR[par])
            k.wait("scalar", "TE", te_y_done)
            k.wait("scalar", "VE", cdec_ve)
            k.op("scalar", lambda e, par=par: e.activation(r_cdec, r_rep(par), AF.Exp), [("SE", 1)])
            rep_se[par] = k.cnt["SE"]
            k.wait("scalar", "VE", ve_xsd)
            k.op("scalar", lambda e, c=c: e.activation(s_dsT, s_revT[:, c * 64:(c + 1) * 64], AF.Exp), [("SE", 1)])
            se_ds = k.cnt["SE"]

            CT_c = s_CT[:, c * 64:(c + 1) * 64]
            BT_c = s_BT[:, c * 64:(c + 1) * 64]
            k.wait("vector", "SE", rep_se[par])
            k.op("vector", lambda e, CT_c=CT_c: TT(
                e, r_cdec.rearrange("p (h t) -> p h t", h=64),
                CT_c.unsqueeze(1).broadcast_to([128, 64, 64]),
                r_cdec.rearrange("p (h t) -> p h t", h=64), OP.mult), [("VE", 1)])
            cdec_ve = k.cnt["VE"]

            k.wait("tensor", "VE")
            k.op("tensor", lambda e, BT_c=BT_c, CT_c=CT_c:
                 e.matmul(bank(6, pn=64, cn=64), BT_c, CT_c, start=True, stop=True), [("TE", 1)])
            te_cb = k.cnt["TE"]

            k.wait("vector", LR[par])
            k.wait("vector", "TE", te_y_done)
            cAT_c = s_cAT[:, c * 64:(c + 1) * 64]
            k.op("vector", lambda e, par=par, cAT_c=cAT_c: TT(
                e, r_ssL.rearrange("p (h t) -> p h t", h=64),
                r_rep(par)[0:64, :].rearrange("p (h t) -> p h t", h=64),
                cAT_c.unsqueeze(2).broadcast_to([64, 64, 64]), OP.subtract))
            k.op("vector", lambda e: e.tensor_scalar_min(r_ssL, r_ssL, 0.0), [("VE", 1)])
            rep_ve[par] = k.cnt["VE"]
            k.wait("scalar", "VE", rep_ve[par])
            k.op("scalar", lambda e: e.activation(r_ssL, r_ssL, AF.Exp), [("SE", 1)])
            se_L = k.cnt["SE"]
            k.wait("vector", "TE", te_cb)
            k.op("vector", lambda e: TT(e, s_cbm, bank(6, pn=64, cn=64), c_mask64, OP.mult))
            vsep()
            k.wait("vector", "SE", se_L)
            k.op("vector", lambda e: TT(
                e, r_ssL.rearrange("p (h t) -> p h t", h=64),
                r_ssL.rearrange("p (h t) -> p h t", h=64),
                s_cbm.unsqueeze(1).broadcast_to([64, 64, 64]), OP.mult), [("VE", 1)])
            if DEBUG and c == 8:
                k.wait("gpsimd", "VE")
                k.store(dbg_d[0:64, 4096:8192], r_ssL)
            k.wait("vector", "LX", xc_ld)
            dt2_c = s_dt2c[:, c * 64:(c + 1) * 64]
            k.op("vector", lambda e, dt2_c=dt2_c: TT(
                e, r_Xs.rearrange("p (h w) -> p h w", h=64),
                r_Xc.rearrange("p (h w) -> p h w", h=64),
                dt2_c.unsqueeze(2).broadcast_to([64, 64, 64]), OP.mult), [("VE", 1)])
            ve_mtxs = k.cnt["VE"]

            k.wait("tensor", "VE", ve_mtxs)
            k.wait("tensor", "VE", ve_state_done)
            for half in range(2):
                pb = pBig0 if half == 0 else pBig1
                k.wait("tensor", "VE", ve_evac[half])
                for j in range(32):
                    hh = half * 32 + j
                    k.op("tensor", lambda e, pb=pb, j=j, hh=hh:
                         e.matmul(pb[0:64, j * 64:(j + 1) * 64],
                                  r_ssL[:, hh * 64:(hh + 1) * 64],
                                  r_Xs[:, hh * 64:(hh + 1) * 64],
                                  start=True, stop=False))
                    k.op("tensor", lambda e, pb=pb, j=j, hh=hh:
                         e.matmul(pb[0:64, j * 64:(j + 1) * 64],
                                  r_cdec[:, hh * 64:(hh + 1) * 64],
                                  state_t[:, hh * 64:(hh + 1) * 64],
                                  start=False, stop=True),
                         [("TE", 1)] if j == 31 else [])
                te_half = k.cnt["TE"]
                k.wait("vector", "TE", te_half)
                k.wait("vector", "GD")
                k.op("vector", lambda e, pb=pb, half=half:
                     e.tensor_copy(r_Yst[:, half * 2048:(half + 1) * 2048], pb[0:64, :]), [("VE", 1)])
                ve_evac[half] = k.cnt["VE"]
            te_y_done = k.cnt["TE"]
            k.wait("gpsimd", "VE", ve_evac[1])
            k.store(Y_d[c * 64:(c + 1) * 64, :], r_Yst)

            k.wait("vector", "SE", se_ds)
            k.op("vector", lambda e: TT(
                e, r_Xs.rearrange("p (h w) -> p h w", h=64),
                r_Xs.rearrange("p (h w) -> p h w", h=64),
                s_dsT.unsqueeze(2).broadcast_to([64, 64, 64]), OP.mult), [("VE", 1)])
            ve_xsd = k.cnt["VE"]

            k.wait("tensor", "VE", ve_xsd)
            Btm_c = s_Btm[:, c * 128:(c + 1) * 128]
            for half in range(2):
                pb = pBig0 if half == 0 else pBig1
                k.wait("tensor", "VE", ve_evac[half])
                for pc in range(4):
                    k.op("tensor", lambda e, pb=pb, half=half, pc=pc, Btm_c=Btm_c:
                         e.matmul(pb[:, pc * 512:(pc + 1) * 512], Btm_c,
                                  r_Xs[:, half * 2048 + pc * 512:half * 2048 + (pc + 1) * 512],
                                  start=True, stop=True),
                         [("TE", 1)] if (half == 1 and pc == 3) else [])
            te_k_done = k.cnt["TE"]

            k.wait("vector", "LCD", cdr_ld)
            k.wait("vector", "TE", te_k_done)
            k.op("vector", lambda e: TT(e, state_t[:], state_t[:], r_cdr, OP.mult))
            k.op("vector", lambda e: TT(e, state_t[:, 0:2048], state_t[:, 0:2048], pBig0[:, :], OP.add))
            k.op("vector", lambda e: TT(e, state_t[:, 2048:4096], state_t[:, 2048:4096], pBig1[:, :], OP.add), [("VE", 1)])
            ve_state_done = k.cnt["VE"]
        if DEBUG:
            k.wait("gpsimd", "VE", ve_state_done)
            k.store(dbg_d[:, 0:4096], state_t[:])
        k.wait("sync", "GD")

    if stage >= 8:
        k.wait("gpsimd", "VE")
        k.store(ag_in_d, state_t[:])
        k.wait("gpsimd", "GD")
        k.op("gpsimd", lambda e: e.collective_compute(
            "AllGather", OP.bypass,
            replica_groups=[[0, 1], [2, 3], [4, 5], [6, 7]],
            ins=[ag_in_d.opt()], outs=[ag_out_d.opt()]), [("CC", 1)])
        k.wait("sync", "CC")
        k.load(state_t[:], ag_out_d[0:128, :])
        k.wait("vector", "LD")
        k.op("vector", lambda e: e.tensor_scalar_mul(state_t[:], state_t[:], c_sel), [("VE", 1)])
        ve_sin = k.cnt["VE"]

    if stage >= 9:
        r_Yh = R(0, 2048)
        r_Xh = R(2048, 2048)
        r_szh = R(4096, 2048)
        r_Dex = R(6144, 4096)
        r_rg = lambda par: R(10240 + 4096 * par, 4096)
        r_cg = R(18432, 4096)
        r_Ys = R(22528, 2048)
        r_Tst = R(24576, 2048)
        phase_guard()
        k.load(r_Dex, dvec.unsqueeze(0).broadcast_to([128, 4096]), sem="LA0")
        rg_se = [0, 0]
        cg_ve = 0
        te_corr = 0
        ys_ve = 0
        prev_tr_ve = 0
        for tt in range(NT):
            for half in range(2):
                hp0 = half * 2048
                k.wait("sync", "VE", ys_ve)
                k.load(r_Yh, Y_d[tt * 128:(tt + 1) * 128, hp0:hp0 + 2048], sem="LA0")
                k.load(r_Xh, X_d[tt * 128:(tt + 1) * 128, hp0:hp0 + 2048], sem="LA0")
                k.load(r_szh, sz_d[tt * 128:(tt + 1) * 128, hp0:hp0 + 2048], sem="LA0")
                ld_yxs = k.cnt["LA0"]
                for ci in range(2):
                    c = tt * 2 + ci
                    par = c % 2
                    k.wait("sync", "SE", rg_se[par])
                    k.load(r_rg(par), gflat_d[c, :].unsqueeze(0).broadcast_to([128, 4096]), sem=LR[par])
                    k.wait("scalar", LR[par])
                    k.wait("scalar", "VE", cg_ve)
                    k.wait("scalar", "TE", te_corr)
                    k.op("scalar", lambda e, par=par: e.activation(r_cg, r_rg(par), AF.Exp), [("SE", 1)])
                    rg_se[par] = k.cnt["SE"]
                    k.wait("vector", "SE", rg_se[par])
                    CT_c = s_CT[:, c * 64:(c + 1) * 64]
                    k.op("vector", lambda e, CT_c=CT_c: TT(
                        e, r_cg.rearrange("p (h t) -> p h t", h=64),
                        CT_c.unsqueeze(1).broadcast_to([128, 64, 64]),
                        r_cg.rearrange("p (h t) -> p h t", h=64), OP.mult), [("VE", 1)])
                    cg_ve = k.cnt["VE"]
                    k.wait("tensor", "VE", cg_ve)
                    if tt == 0 and half == 0 and ci == 0:
                        k.wait("tensor", "VE", ve_sin)
                    k.wait("tensor", "VE", ys_ve)
                    k.wait("tensor", "VE", prev_tr_ve)
                    pb = pBig0 if ci == 0 else pBig1
                    pofs = 0 if ci == 0 else 64
                    tp = None if ci == 0 else (0, 64)
                    for j in range(32):
                        hh = half * 32 + j
                        k.op("tensor", lambda e, pb=pb, j=j, hh=hh, pofs=pofs, tp=tp:
                             e.matmul(pb[pofs:pofs + 64, j * 64:(j + 1) * 64],
                                      r_cg[:, hh * 64:(hh + 1) * 64],
                                      state_t[:, hh * 64:(hh + 1) * 64],
                                      start=True, stop=True,
                                      tile_position=tp),
                             [("TE", 1)] if j == 31 else [])
                    te_corr = k.cnt["TE"]
                k.wait("vector", "LA0", ld_yxs)
                k.wait("vector", "TE", te_corr)
                for ci in range(2):
                    pb = pBig0 if ci == 0 else pBig1
                    pofs = 0 if ci == 0 else 64
                    rows = slice(ci * 64, ci * 64 + 64)
                    k.op("vector", lambda e, rows=rows, hp0=hp0: TT(
                        e, r_Ys[rows, :], r_Xh[rows, :], r_Dex[rows, hp0:hp0 + 2048], OP.mult))
                    k.op("vector", lambda e, rows=rows: TT(e, r_Ys[rows, :], r_Ys[rows, :], r_Yh[rows, :], OP.add))
                    k.op("vector", lambda e, rows=rows, pb=pb, pofs=pofs: TT(
                        e, r_Ys[rows, :], r_Ys[rows, :], pb[pofs:pofs + 64, :], OP.add))
                    k.op("vector", lambda e, rows=rows: TT(e, r_Ys[rows, :], r_Ys[rows, :], r_szh[rows, :], OP.mult), [("VE", 1)])
                ys_ve = k.cnt["VE"]
                k.wait("tensor", "VE", ys_ve)
                grp_ve = {0: 0, 1: 0}
                for g in range(4):
                    bk = bank(6 + g % 2)
                    k.wait("tensor", "VE", grp_ve[g % 2])
                    for j in range(4):
                        di = g * 4 + j
                        k.op("tensor", lambda e, bk=bk, j=j, di=di:
                             e.transpose(bk[:, j * 128:(j + 1) * 128], r_Ys[:, di * 128:(di + 1) * 128], c_ident), [("TE", 1)])
                    k.wait("vector", "TE")
                    k.wait("vector", "GD")
                    k.op("vector", lambda e, bk=bk, g=g: e.tensor_copy(r_Tst[:, g * 512:(g + 1) * 512], bk), [("VE", 1)])
                    grp_ve[g % 2] = k.cnt["VE"]
                ve_tr = k.cnt["VE"]
                prev_tr_ve = ve_tr
                k.wait("gpsimd", "VE", ve_tr)
                k.store(YsT_d.rearrange("(dt p) t -> p dt t", p=128)[:, half * 16:half * 16 + 16, tt * 128:(tt + 1) * 128],
                        r_Tst[:].rearrange("p (dt c) -> p dt c", c=128))
        k.wait("sync", "GD")

    if stage >= 10:
        phase_guard()
        r_ysth = R(0, 16384)
        r_wsl = lambda par: R(16384 + 8192 * par, 8192)
        wsl_te = [0, 0]
        xm_ve = {0: 0, 1: 0, 2: 0, 3: 0}
        for th in range(2):
            if th > 0:
                k.wait("sync", "TE")
            k.load(r_ysth.rearrange("p (dt t) -> p dt t", t=512),
                   YsT_d.rearrange("(dt p) t -> p dt t", p=128)[:, :, th * 512:(th + 1) * 512],
                   sem="LR0")
            for oc in range(8):
                par = oc % 2
                k.wait("sync", "TE", wsl_te[par])
                k.load(r_wsl(par).rearrange("p (dt o) -> p dt o", o=256),
                       woutT.rearrange("(dt p) o -> p dt o", p=128)[:, :, oc * 256:(oc + 1) * 256],
                       sem=LW[par])
                k.wait("tensor", LW[par])
                if oc == 0:
                    k.wait("tensor", "LR0")
                for tq in range(4):
                    tt = th * 4 + tq
                    bk = bank(tq, c0=(oc % 2) * 256, cn=256)
                    if oc % 2 == 0:
                        k.wait("tensor", "VE", xm_ve[tq])
                    for di in range(NDI):
                        k.op("tensor", lambda e, bk=bk, di=di, tq=tq, par=par:
                             e.matmul(bk, r_ysth[:, di * 512 + tq * 128:di * 512 + (tq + 1) * 128],
                                      r_wsl(par)[:, di * 256:(di + 1) * 256],
                                      start=(di == 0), stop=(di == NDI - 1)),
                             [("TE", 1)] if (tq == 3 and di == NDI - 1) else [])
                wsl_te[par] = k.cnt["TE"]
                if oc % 2 == 1:
                    te_b = k.cnt["TE"]
                    for tq in range(4):
                        tt = th * 4 + tq
                        k.wait("sync", "VE", xm_ve[tq])
                        k.load(stg[:, tq * 512:tq * 512 + 512],
                               xin[tt * 128:(tt + 1) * 128, (oc - 1) * 256:(oc + 1) * 256], sem="LA0")
                        ld_x = k.cnt["LA0"]
                        k.wait("vector", "TE", te_b)
                        k.wait("vector", "LA0", ld_x)
                        k.wait("vector", "GD")
                        k.op("vector", lambda e, tq=tq: TT(e, stg2[:, tq * 512:tq * 512 + 512],
                                                           bank(tq), stg[:, tq * 512:tq * 512 + 512], OP.add), [("VE", 1)])
                        xm_ve[tq] = k.cnt["VE"]
                        k.wait("gpsimd", "VE", xm_ve[tq])
                        k.store(xmid_d[tt * 128:(tt + 1) * 128, (oc - 1) * 256:(oc + 1) * 256],
                                stg2[:, tq * 512:tq * 512 + 512])
        k.wait("sync", "GD")

    if stage >= 11:
        phase_guard()
        k.op("vector", lambda e: e.memset(c_gmx, 0.0), [("VE", 1)])
        for tt in range(NT):
            par = tt % 2
            k.wait("sync", "SE")
            k.load(r_x(par), xmid_d[tt * 128:(tt + 1) * 128, :], sem="LA0")
            k.wait("vector", "LA0")
            xt = r_x(par)
            k.op("vector", lambda e, xt=xt: e.reduce_sum(c_t0, xt, axis=AX.X))
            vsep()
            k.op("vector", lambda e, tt=tt: e.tensor_scalar_mul(c_m2(tt), c_t0, 1.0 / D))
            k.op("vector", lambda e, xt=xt: e.reduce_max(c_mx, xt, axis=AX.X))
            k.op("vector", lambda e, xt=xt: e.tensor_reduce(c_mn, xt, AX.X, OP.min))
            vsep()
            k.op("vector", lambda e, xt=xt, tt=tt, par=par: e.tensor_scalar_sub(r_dd(par), xt, c_m2(tt)))
            k.op("vector", lambda e, par=par: TT(e, r_ts(par), r_dd(par), r_dd(par), OP.mult))
            k.op("vector", lambda e, par=par: e.reduce_sum(c_vv, r_ts(par), axis=AX.X))
            vsep()
            k.op("vector", lambda e: e.tensor_scalar_mul(c_vv, c_vv, 1.0 / D), [("VE", 1)])
            k.wait("scalar", "VE")
            k.op("scalar", lambda e: e.activation(c_t1, c_vv, AF.Sqrt, bias=c_eps), [("SE", 1)])
            k.wait("vector", "SE")
            k.op("vector", lambda e, tt=tt: e.reciprocal(c_rs2(tt), c_t1))
            vsep()
            k.op("vector", lambda e, tt=tt: TT(e, c_mx, c_mx, c_m2(tt), OP.subtract))
            vsep()
            k.op("vector", lambda e, tt=tt: TT(e, c_mn, c_m2(tt), c_mn, OP.subtract))
            vsep()
            k.op("vector", lambda e: TT(e, c_mx, c_mx, c_mn, OP.max))
            vsep()
            k.op("vector", lambda e, tt=tt: TT(e, c_mx, c_mx, c_rs2(tt), OP.mult))
            vsep()
            k.op("vector", lambda e: TT(e, c_gmx, c_gmx, c_mx, OP.max), [("VE", 1)])
        k.wait("gpsimd", "VE")
        k.store(smax_d[0:128], c_gmx)
        k.wait("sync", "GD")
        k.load(stg[0:1, 0:128], smax_d[0:128].unsqueeze(0))
        k.wait("vector", "LD")
        k.op("vector", lambda e: e.reduce_max(stg[0:1, 128:129], stg[0:1, 0:128], axis=AX.X))
        vsep()
        k.op("vector", lambda e: e.tensor_copy(stg[0:1, 136:144], stg[0:1, 128:129].broadcast_to([1, 8])), [("VE", 1)])
        k.wait("gpsimd", "VE")
        k.store(mx_in_d, stg[0:1, 136:144])
        k.wait("gpsimd", "GD")
        k.op("gpsimd", lambda e: e.collective_compute(
            "AllReduce", OP.max, replica_groups=[list(range(CC_WORLD))],
            ins=[mx_in_d.opt()], outs=[mx_out_d.opt()]), [("CC", 1)])
        k.wait("sync", "CC")
        k.load(stg[0:1, 144:152], mx_out_d)
        k.wait("vector", "LD")
        k.op("vector", lambda e: e.tensor_scalar(stg[0:1, 152:153], stg[0:1, 144:145], 1.0 / 448.0, 1e-7, OP.mult, OP.max))
        vsep()
        k.op("vector", lambda e: e.reciprocal(stg[0:1, 153:154], stg[0:1, 152:153]), [("VE", 1)])
        k.wait("gpsimd", "VE")
        k.store(sc_d[0:1, 0:1], stg[0:1, 153:154])
        k.wait("sync", "GD")
        k.load(c_invsc, sc_d[0:1, 0:1].broadcast_to([128, 1]))
        k.wait("vector", "LD")
        k.mark("vector", "VE")

    if stage >= 12:
        phase_guard()
        xq_area = R(0, 8192)
        s_xq = xq_area.bitcast(BF16)
        r_q = R(8192, 2048)
        r_xa = R(10240, 2048)
        r_da = R(12288, 2048)
        ve_q = 0
        for tt in range(NT):
            k.wait("sync", "VE", ve_q)
            k.load(r_xa, xmid_d[tt * 128:(tt + 1) * 128, :], sem="LA0")
            k.wait("vector", "LA0")
            k.op("vector", lambda e, tt=tt: e.tensor_scalar(r_da, r_xa, c_m2(tt), c_rs2(tt), OP.subtract, OP.mult), [("VE", 1)])
            k.wait("tensor", "VE")
            grp_ve = {0: 0, 1: 0}
            for g in range(4):
                bk = bank(0 + g % 2)
                k.wait("tensor", "VE", grp_ve[g % 2])
                for j in range(4):
                    di = g * 4 + j
                    k.op("tensor", lambda e, bk=bk, j=j, di=di:
                         e.transpose(bk[:, j * 128:(j + 1) * 128], r_da[:, di * 128:(di + 1) * 128], c_ident), [("TE", 1)])
                k.wait("vector", "TE")
                k.op("vector", lambda e, bk=bk, g=g: e.tensor_copy(r_q[:, g * 512:(g + 1) * 512], bk), [("VE", 1)])
                grp_ve[g % 2] = k.cnt["VE"]
            a = r_q
            ai = a.bitcast(I32)
            sb_ = stg[:, 0:2048]
            sbi = sb_.bitcast(I32)
            s2 = stg2[:, 0:2048]
            s2i = s2.bitcast(I32)
            k.op("vector", lambda e, a=a: e.tensor_scalar_mul(a, a, c_invsc))
            k.op("vector", lambda e, ai=ai, sbi=sbi: e.tensor_single_scalar(sbi, ai, 0x7F800000, OP.bitwise_and))
            k.op("vector", lambda e, sbi=sbi: e.tensor_single_scalar(sbi, sbi, 3 << 23, OP.subtract))
            k.op("vector", lambda e, sbi=sbi: e.tensor_single_scalar(sbi, sbi, 118 << 23, OP.max))
            k.op("vector", lambda e, s2i=s2i, sbi=sbi: e.tensor_scalar(s2i, sbi, -1, 0x7F000000, OP.mult, OP.add))
            k.op("vector", lambda e, a=a, s2=s2: TT(e, a, a, s2, OP.mult))
            k.op("vector", lambda e, a=a: e.tensor_scalar(a, a, C15, -C15, OP.add, OP.add))
            k.op("vector", lambda e, a=a, sb_=sb_: TT(e, a, a, sb_, OP.mult))
            k.op("vector", lambda e, tt=tt, a=a: e.tensor_copy(
                s_xq.rearrange("p (dt c) -> p dt c", c=1024)[:, :, tt * 128:(tt + 1) * 128],
                a.rearrange("p (dt c) -> p dt c", c=128)), [("VE", 1)])
            ve_q = k.cnt["VE"]
        if DEBUG:
            k.wait("gpsimd", "VE", ve_q)
            k.store(dbg_d[:, 4096:6144], xq_area[:, 0:2048])

    if stage >= 13:
        phase_guard()
        r_fs = lambda par: R(14336 + 4096 * par, 4096)
        r_fb = R(22528, 2048)
        k.load(r_fb, fbias.unsqueeze(0).broadcast_to([128, D]), sem="LC")
        fb_ld = k.cnt["LC"]
        fs_te = [0, 0]
        out_ve = {0: 0, 1: 0}
        ve_q_all = k.cnt["VE"]
        k.wait("tensor", "VE", ve_q_all)
        for opi in range(4):
            par = opi % 2
            k.wait("gpsimd", "TE", fs_te[par])
            fslab = r_fs(par).bitcast(BF16)
            k.op("gpsimd", lambda e, fslab=fslab, opi=opi: e.dma_start(
                out=fslab[:, 0:8192].rearrange("p (dt o) -> p dt o", o=512),
                in_=fwT.rearrange("(dt p) o -> p dt o", p=128)[:, :, opi * 512:(opi + 1) * 512]), [("GF", 16)])
            gf_ld = k.cnt["GF"]
            for tt in range(NT):
                bk = bank(0 + tt % 2)
                k.wait("tensor", "GF", gf_ld)
                k.wait("tensor", "VE", out_ve[tt % 2])
                for di in range(ND):
                    k.op("tensor", lambda e, bk=bk, di=di, tt=tt, fslab=fslab:
                         e.matmul(bk, s_xq.rearrange("p (dt c) -> p dt c", c=1024)[:, di, tt * 128:(tt + 1) * 128],
                                  fslab[:, di * 512:(di + 1) * 512],
                                  start=(di == 0), stop=(di == ND - 1)),
                         [("TE", 1)] if di == ND - 1 else [])
                te_f = k.cnt["TE"]
                xm_t = stg[:, 512 + (tt % 2) * 512:512 + (tt % 2) * 512 + 512]
                k.wait("sync", "VE", out_ve[tt % 2])
                k.load(xm_t, xmid_d[tt * 128:(tt + 1) * 128, opi * 512:(opi + 1) * 512], sem=("LA0" if tt % 2 == 0 else "LA1"))
                ld_xm = k.cnt["LA0" if tt % 2 == 0 else "LA1"]
                k.wait("vector", "TE", te_f)
                k.wait("vector", "LA0" if tt % 2 == 0 else "LA1", ld_xm)
                k.wait("vector", "LC", fb_ld)
                k.wait("vector", "GD")
                ot_t = stg2[:, (tt % 2) * 512:(tt % 2) * 512 + 512]
                k.op("vector", lambda e, bk=bk, xm_t=xm_t, ot_t=ot_t: e.scalar_tensor_tensor(
                    ot_t, bk, c_ws, xm_t, OP.mult, OP.add))
                k.op("vector", lambda e, opi=opi, ot_t=ot_t: TT(e, ot_t, ot_t, r_fb[:, opi * 512:(opi + 1) * 512], OP.add), [("VE", 1)])
                out_ve[tt % 2] = k.cnt["VE"]
                k.wait("gpsimd", "VE", out_ve[tt % 2])
                k.store(out[tt * 128:(tt + 1) * 128, opi * 512:(opi + 1) * 512], ot_t)
            fs_te[par] = k.cnt["TE"]

    k.wait("gpsimd", "GD")
    k.wait("gpsimd", "GF")

    with es:
        with nc.Block() as block:
            @block.sync
            def _(e):
                for f in k.q["sync"]:
                    f(e)

            @block.tensor
            def _(e):
                for f in k.q["tensor"]:
                    f(e)

            @block.vector
            def _(e):
                for f in k.q["vector"]:
                    f(e)

            @block.scalar
            def _(e):
                for f in k.q["scalar"]:
                    f(e)

            @block.gpsimd
            def _(e):
                for f in k.q["gpsimd"]:
                    f(e)
    return nc


def make_in_maps(inputs):
    f32 = np.float32
    x = np.ascontiguousarray(np.asarray(inputs["L0_attn_x"], f32))
    winT = np.ascontiguousarray(np.asarray(inputs["L0_attn_in_proj_w"], f32).T)
    convw = np.ascontiguousarray(np.asarray(inputs["L0_attn_conv_w"], f32))
    convb = np.ascontiguousarray(np.asarray(inputs["L0_attn_conv_b"], f32))
    wxT = np.ascontiguousarray(np.asarray(inputs["L0_attn_x_proj_w"], f32).T)
    wdtT = np.ascontiguousarray(np.asarray(inputs["L0_attn_dt_proj_w"], f32).T)
    dtb = np.ascontiguousarray(np.asarray(inputs["L0_attn_dt_proj_b"], f32))
    alog = np.ascontiguousarray(np.asarray(inputs["L0_attn_A_log"], f32))
    dvec = np.ascontiguousarray(np.repeat(np.asarray(inputs["L0_attn_D"], f32), P_))
    woutT = np.ascontiguousarray(np.asarray(inputs["L0_attn_out_proj_w"], f32).T)
    fwT = np.ascontiguousarray(np.asarray(inputs["L0_ffn_weight"], f32).T)
    fb = np.ascontiguousarray(np.asarray(inputs["L0_ffn_bias"], f32))
    wsc = np.asarray(inputs["L0_ffn_weight_scale"], f32).reshape(1, 1)
    ident = np.eye(128, dtype=f32)
    ar = np.arange(64)
    uincl = (ar[:, None] <= ar[None, :]).astype(f32)
    ustrict = (ar[:, None] > ar[None, :]).astype(f32)
    mask64 = (ar[None, :] >= ar[:, None]).astype(f32)
    maps = []
    for core in range(8):
        b, hh = core // 2, core % 2
        t0 = hh * T
        xi = np.ascontiguousarray(x[b, t0:t0 + T])
        xpr = np.ascontiguousarray(x[b, t0 - 3:t0]) if hh == 1 else np.zeros((3, D), f32)
        sel = np.full((128, 1), float(hh), f32)
        maps.append(dict(xin=xi, xprev=xpr, winT=winT, convw=convw, convb=convb,
                         wxT=wxT, wdtT=wdtT, dtbias=dtb, alog=alog, dvec=dvec,
                         woutT=woutT, fwT=fwT, fbias=fb, wsc=wsc, selc=sel,
                         ident=ident, uincl=uincl, ustrict=ustrict, mask64=mask64))
    return maps


def kernel(**inputs):
    nc = build_program(13)
    maps = make_in_maps(inputs)
    res = run_bass_kernel_spmd(nc, maps, core_ids=list(range(8)))
    outf = np.zeros((4, 2048, D), np.float32)
    for core in range(8):
        b, hh = core // 2, core % 2
        outf[b, hh * T:(hh + 1) * T] = res.results[core]["out"]
    return outf
